# version 01
# baseline (speedup 1.0000x reference)
import contextlib
import numpy as np
import ml_dtypes
import concourse.bass as bass
import concourse.mybir as mybir
from concourse.bass_utils import run_bass_kernel_spmd

F32 = mybir.dt.float32
BF16 = mybir.dt.bfloat16
I32 = mybir.dt.int32
AF = mybir.ActivationFunctionType
ALU = mybir.AluOpType
AX = mybir.AxisListType
BF = ml_dtypes.bfloat16

F = 2048
T = 2048
H = 8192
NCF = 16
NG = 4
TG = 512
EPS = 1e-5


class Ctx:
    def __init__(self):
        self.sems = {}
        self.plan = {}
        self.dry = True
        self.S = {}
        self.totals = {}

    def inc(self, sem, n, mark=None):
        self.sems[sem] = self.sems.get(sem, 0) + n
        if mark is not None:
            if self.dry:
                assert mark not in self.plan, f"dup mark {mark}"
                self.plan[mark] = self.sems[sem]
            else:
                assert self.plan[mark] == self.sems[sem], (
                    f"plan mismatch {mark}: plan={self.plan[mark]} "
                    f"emit={self.sems[sem]}")
        return self.sems[sem]


class E:

    AUTO = {"V": "DVE", "A": "ACT"}

    def __init__(self, ctx, eng, en=None):
        self.ctx = ctx
        self.eng = eng
        self.auto = self.AUTO.get(en)

    def wait(self, sem, mark):
        if self.ctx.dry:
            assert isinstance(mark, str)
            return
        v = self.ctx.plan[mark]
        if v > 0:
            self.eng.wait_ge(self.ctx.S[sem], v)

    def wait_total(self, sem):
        if self.ctx.dry:
            return
        v = self.ctx.totals.get(sem, 0)
        if v > 0:
            self.eng.wait_ge(self.ctx.S[sem], v)

    def dma(self, out, in_, sem, mark=None, **kw):
        if self.ctx.dry:
            self.ctx.inc(sem, 16, mark)
            return
        ins = self.eng.dma_start(out=out, in_=in_, **kw)
        ins.then_inc(self.ctx.S[sem], 16)
        self.ctx.inc(sem, 16, mark)

    def cc(self, kind, op, groups, in_ap, out_ap, sem, mark=None):
        if self.ctx.dry:
            self.ctx.inc(sem, 1, mark)
            return
        ins = self.eng.collective_compute(
            kind, op, replica_groups=groups, ins=[in_ap], outs=[out_ap])
        ins.then_inc(self.ctx.S[sem])
        self.ctx.inc(sem, 1, mark)

    def op(self, fn, *args, sem=None, mark=None, **kw):
        if self.auto is not None:
            prev = self.ctx.sems.get(self.auto, 0)
            if not self.ctx.dry and prev > 0:
                self.eng.wait_ge(self.ctx.S[self.auto], prev)
            if sem is None:
                sem = self.auto
            else:
                assert sem == self.auto, (sem, self.auto)
        if self.ctx.dry:
            if sem is not None:
                self.ctx.inc(sem, 1, mark)
            return
        ins = getattr(self.eng, fn)(*args, **kw)
        if sem is not None:
            ins.then_inc(self.ctx.S[sem])
            self.ctx.inc(sem, 1, mark)

    def mm(self, out, lhsT, rhs, start, stop, sem=None, mark=None):
        self.op("matmul", out, lhsT, rhs, start=start, stop=stop,
                sem=sem, mark=mark)


def build_nc(n_phases=3):
    nc = bass.Bass("TRN2", target_bir_lowering=False, debug=False)

    dIN = {}
    def din(name, shape, dt):
        dIN[name] = nc.dram_tensor(name, shape, dt, kind="ExternalInput")
        return dIN[name]

    din("x_in", [F, T], F32)
    din("wupqT", [256, H], I32)
    din("wdnqT", [1024, F], I32)
    din("s_up_p", [2, 2, 4, 2048], F32)
    din("s_dn_p", [8, 2, F], F32)
    din("wqkvo_sl", [1024, F], F32)
    din("b_up_t", [128, 64], F32)
    din("b_dn_t", [128, 16], F32)
    din("AupT_sl", [16, 256], F32)
    din("Bup", [16, H], F32)
    din("AdnT_sl", [16, 1024], F32)
    din("Bdn", [16, F], F32)
    din("ones_col", [128, 1], BF16)
    din("ones_row", [1, 128], F32)
    din("exp2", [2, 128], F32)
    din("exp2h", [2, 128], F32)
    din("constv", [128, 2], F32)

    y_out = nc.dram_tensor("y_out", [F, T], F32, kind="ExternalOutput")

    wup_c = [nc.dram_tensor(f"wup_c{r}", [256, 1024], BF16) for r in range(8)]
    wup_g = [nc.dram_tensor(f"wup_g{r}", [F, 1024], BF16) for r in range(8)]
    wdn_c = [nc.dram_tensor(f"wdn_c{r}", [1024, 512], BF16) for r in range(4)]
    wdn_g = [nc.dram_tensor(f"wdn_g{r}", [H, 512], BF16) for r in range(4)]
    wqkvo_c = nc.dram_tensor("wqkvo_c", [1024, F], BF16)
    wqkvo_g = nc.dram_tensor("wqkvo_g", [H, F], BF16)
    x_a = nc.dram_tensor("x_a", [F, T], F32)
    x_b = nc.dram_tensor("x_b", [F, T], F32)
    x1_d = nc.dram_tensor("x1_d", [F, T], BF16)
    sc_p = nc.dram_tensor("sc_p", [F, F], F32)
    sc_r = nc.dram_tensor("sc_r", [F, F], F32)
    M_d = nc.dram_tensor("M_d", [F, F], BF16)

    ctx = Ctx()
    W_SEMS = (["wCV0", "wCV1", "wSP0", "wSP1", "wSPD", "wUQ0", "wUQ1",
               "wUP0", "wUP1", "wDN0", "wDN1", "wW", "wM0", "wM1"])
    A_SEMS = (["aCON", "aCVW0", "aCVW1", "aWR0", "aWR1", "aXG", "aXR0", "aXR1",
               "aX1W0", "aX1W1", "aX1R0", "aX1R1", "aXV0", "aXV1", "aSC",
               "aSM0", "aSM1", "aMW0", "aMW1", "aYW0", "aYW1"])
    sem_names = (W_SEMS + A_SEMS + ["PE", "ACT", "DVE"]
                 + [f"ccU{r}" for r in range(8)]
                 + [f"ccD{r}" for r in range(4)]
                 + ["ccQ"] + [f"ccS{r}" for r in range(4)])

    with contextlib.ExitStack() as st:
        R1 = st.enter_context(nc.sbuf_tensor("R1", [128, 32768], BF16))
        R2 = st.enter_context(nc.sbuf_tensor("R2", [128, 32768], BF16))
        R3 = st.enter_context(nc.sbuf_tensor("R3", [128, 32768], BF16))
        onec = st.enter_context(nc.sbuf_tensor("onec", [128, 1], BF16))
        oner = st.enter_context(nc.sbuf_tensor("oner", [1, 128], F32))
        exp2 = st.enter_context(nc.sbuf_tensor("exp2_sb", [2, 128], F32))
        exp2h = st.enter_context(nc.sbuf_tensor("exp2h_sb", [2, 128], F32))
        bupt = st.enter_context(nc.sbuf_tensor("bupt", [128, 64], F32))
        bdnt = st.enter_context(nc.sbuf_tensor("bdnt", [128, 16], F32))
        bdnh = st.enter_context(nc.sbuf_tensor("bdnh", [128, 16], F32))
        xres = [st.enter_context(nc.sbuf_tensor(f"xres{i}", [128, 512], F32))
                for i in range(2)]
        x1tb = [st.enter_context(nc.sbuf_tensor(f"x1tb{i}", [128, 2048], BF16))
                for i in range(2)]
        constv = st.enter_context(nc.sbuf_tensor("constv_sb", [128, 2], F32))
        nmax = st.enter_context(nc.sbuf_tensor("nmax", [128, 1], F32))
        mstg = [None, None]
        rs_c = st.enter_context(nc.sbuf_tensor("rs_c", [128, 1], F32))
        rr_all = st.enter_context(nc.sbuf_tensor("rr_all", [128, 16], F32))
        PS = [st.enter_context(nc.psum_tensor(f"ps{i}", [128, 512], F32))
              for i in range(8)]
        for s in sem_names:
            ctx.S[s] = st.enter_context(nc.semaphore(s))

        def bf(a, col, n):
            return a[:, col:col + n]

        def f32v(a, col, n):
            return a[:, col:col + 2 * n].bitcast(F32)

        def i32v(a, col, n):
            return a[:, col:col + 2 * n].bitcast(I32)

        mstg[0] = xres[0][:, 0:256].bitcast(BF16)
        mstg[1] = xres[1][:, 0:256].bitcast(BF16)

        nc.const_aps.register(F32, -7.5, constv[:, 0:1])
        nc.const_aps.register(F32, EPS, constv[:, 1:2])

        xg = lambda c: f32v(R2, c * 1024, 512)
        x1g = lambda c: bf(R2, 16384 + c * 512, 512)
        RB = 24576
        row_m = f32v(R2, RB, 512)[0:1, :]
        row_v = f32v(R2, RB + 1024, 512)[0:1, :]
        row_r = f32v(R2, RB + 2048, 512)[0:1, :]
        row_mr = f32v(R2, RB + 3072, 512)[0:1, :]
        xb_s = lambda i: bf(R2, RB + 4096 + i * 512, 512)
        sq_s = lambda i: bf(R2, RB + 5120 + i * 512, 512)
        x1st = lambda i: bf(R2, 16384 + i * 512, 512)
        scst = f32v(R2, 0, 2048)
        smx = lambda i: f32v(R2, 4096 + i * 4096, 2048)
        wvh = bf(R2, 16384, 16384)
        wo_sb = lambda e: bf(R2, e * 2048, 2048)

        upb = lambda s: bf(R3, s * 8192, 8192)
        dnb = lambda s: bf(R3, 16384 + s * 8192, 8192)
        dq_qi = lambda i: i32v(R3, i * 1024, 512)
        dq_qf = lambda i: f32v(R3, 2048 + i * 1024, 512)
        dq_ob = lambda i: bf(R3, 6144 + i * 512, 512)
        dq_sp = f32v(R3, 8192, 4096)
        dq_bd = f32v(R3, 16384, 2048)[0:16, :]
        lAu = f32v(R3, 7168, 256)[0:16, :]
        lAd = f32v(R3, 20480, 1024)[0:16, :]
        lB = [f32v(R3, 22528 + i * 1024, 512)[0:16, :] for i in range(2)]
        x1v = lambda i: bf(R3, i * 8192, 8192)
        wsm = lambda e: bf(R3, e * 2048, 2048)
        cv_in = lambda i: f32v(R1, i * 4096, 2048)
        cv_ob = lambda i: bf(R1, 16384 + i * 2048, 2048)

        def wrow(kind, c):
            return 1024 * (c // 2) + 256 * kind + 128 * (c % 2)

        GROUPS8 = [list(range(8))]
        GROUPS4 = [[0, 1, 2, 3], [4, 5, 6, 7]]

        def p0(en, x):
            s_, g_, t_, a_, v_ = (en == 'S'), (en == 'G'), (en == 'T'), (en == 'A'), (en == 'V')
            if g_:
                x.dma(onec[:, :], dIN["ones_col"].ap(), "aCON")
                x.dma(oner[:, :], dIN["ones_row"].ap(), "aCON")
                x.dma(exp2[:, :], dIN["exp2"].ap(), "aCON")
                x.dma(exp2h[:, :], dIN["exp2h"].ap(), "aCON")
                x.dma(constv[:, :], dIN["constv"].ap(), "aCON")
                x.dma(bupt[:, :], dIN["b_up_t"].ap(), "aCON")
                x.dma(bdnt[:, :], dIN["b_dn_t"].ap(), "aCON")
                x.dma(lAu[:, :], dIN["AupT_sl"].ap(), "aCON")
                x.dma(lAd[:, :], dIN["AdnT_sl"].ap(), "aCON")
                x.dma(dq_bd, dIN["Bdn"].ap(), "aCON", mark="p0.consts")
            if v_:
                x.wait("aCON", "p0.consts")
                x.op("tensor_scalar_mul", bdnh[:, :], bdnt[:, :], 0.5)
                x.op("tensor_scalar_mul", dq_bd, dq_bd, 0.5,
                     sem="DVE", mark="p0.bdnh")

            for i in range(8):
                sl = i % 2
                if s_:
                    if i >= 2:
                        x.wait("DVE", f"p0.cv.{i - 2}")
                    x.dma(cv_in(sl), dIN["wqkvo_sl"].ap()[i * 128:(i + 1) * 128, :],
                          f"wCV{sl}", mark=f"p0.cvl.{i}")
                if v_:
                    x.wait(f"wCV{sl}", f"p0.cvl.{i}")
                    if i >= 2:
                        x.wait(f"aCVW{sl}", f"p0.cvw.{i - 2}")
                    if i < 2:
                        x.op("tensor_scalar_mul", cv_ob(sl), cv_in(sl), 0.125,
                             sem="DVE", mark=f"p0.cv.{i}")
                    else:
                        x.op("tensor_copy", cv_ob(sl), cv_in(sl),
                             sem="DVE", mark=f"p0.cv.{i}")
                if g_:
                    x.wait("DVE", f"p0.cv.{i}")
                    x.dma(wqkvo_c.ap()[i * 128:(i + 1) * 128, :], cv_ob(sl),
                          f"aCVW{sl}", mark=f"p0.cvw.{i}")
            if g_:
                x.wait("aCVW0", "p0.cvw.6")
                x.wait("aCVW1", "p0.cvw.7")
                x.cc("AllGather", ALU.bypass, GROUPS8,
                     wqkvo_c.ap().opt(), wqkvo_g.ap().opt(), "ccQ", mark="p0.agq")

            nblk = 0
            for hr in range(8):
                for c in range(2):
                    if s_ and hr % 2 == 0:
                        if nblk >= 1:
                            x.wait("DVE", f"p0.up.mul.{nblk - 1}")
                        q = hr // 2
                        x.dma(dq_sp[0:1, c * 2048:c * 2048 + 2048],
                              dIN["s_up_p"].ap()[c, 0, q, :].unsqueeze(0), f"wSP{c}")
                        x.dma(dq_sp[1:2, c * 2048:c * 2048 + 2048],
                              dIN["s_up_p"].ap()[c, 1, q, :].unsqueeze(0), f"wSP{c}",
                              mark=f"p0.up.sp.{c}.{hr // 2}")
                    for hb in (2 * hr, 2 * hr + 1):
                        bsl = nblk % 2
                        if s_:
                            if nblk >= 2:
                                x.wait("DVE", f"p0.up.mul.{nblk - 2}")
                            x.dma(dq_qi(bsl),
                                  dIN["wupqT"].ap()[c * 128:(c + 1) * 128,
                                                    hb * 512:(hb + 1) * 512], f"wUQ{bsl}")
                            x.dma(lB[bsl][:, :],
                                  dIN["Bup"].ap()[:, hb * 512:(hb + 1) * 512],
                                  f"wUQ{bsl}", mark=f"p0.up.qi.{nblk}")
                        if t_:
                            x.wait(f"wUQ{bsl}", f"p0.up.qi.{nblk}")
                            if nblk == 0:
                                x.wait("aCON", "p0.consts")
                            if nblk >= 1:
                                x.wait("DVE", f"p0.up.mul.{nblk - 1}")
                            x.wait(f"wSP{c}", f"p0.up.sp.{c}.{hr // 2}")
                            off = c * 2048 + (hb % 4) * 512
                            x.mm(PS[6][:, :], exp2[:, :],
                                 dq_sp[0:2, off:off + 512], True, True)
                            x.mm(PS[7][:, :], lAu[:, c * 128:(c + 1) * 128],
                                 lB[bsl][:, :], True, True,
                                 sem="PE", mark=f"p0.up.pe.{nblk}")
                        if a_:
                            x.wait(f"wUQ{bsl}", f"p0.up.qi.{nblk}")
                            if nblk == 0:
                                x.wait("aCON", "p0.consts")
                            if nblk >= 2:
                                x.wait("DVE", f"p0.up.mul.{nblk - 2}")
                            x.op("activation", dq_qf(bsl), dq_qi(bsl), AF.Copy,
                                 bias=-7.5, sem="ACT", mark=f"p0.up.cv.{nblk}")
                        if v_:
                            x.wait("ACT", f"p0.up.cv.{nblk}")
                            x.wait("PE", f"p0.up.pe.{nblk}")
                            if nblk >= 2:
                                x.wait(f"aWR{bsl}", f"p0.up.wr.{nblk - 2}")
                            x.op("tensor_mul", dq_qf(bsl), dq_qf(bsl), PS[6][:, :])
                            x.op("tensor_add", dq_ob(bsl), dq_qf(bsl), PS[7][:, :],
                                 sem="DVE", mark=f"p0.up.mul.{nblk}")
                        if g_:
                            x.wait("DVE", f"p0.up.mul.{nblk}")
                            x.dma(wup_c[hr].ap()[c * 128:(c + 1) * 128,
                                                 (hb % 2) * 512:(hb % 2) * 512 + 512],
                                  dq_ob(bsl), f"aWR{bsl}", mark=f"p0.up.wr.{nblk}")
                        nblk += 1
                if g_:
                    x.wait("aWR0", f"p0.up.wr.{nblk - 2}")
                    x.wait("aWR1", f"p0.up.wr.{nblk - 1}")
                    x.cc("AllGather", ALU.bypass, GROUPS8,
                         wup_c[hr].ap().opt(), wup_g[hr].ap().opt(),
                         f"ccU{hr}", mark=f"p0.agu.{hr}")

            nb = 0
            for j in range(8):
                if s_:
                    if nb >= 1:
                        x.wait("DVE", f"p0.dn.mul.{nb - 1}")
                    else:
                        x.wait("DVE", "p0.up.mul.31")
                    x.dma(dq_sp[0:1, 0:2048],
                          dIN["s_dn_p"].ap()[j, 0, :].unsqueeze(0), "wSPD")
                    x.dma(dq_sp[1:2, 0:2048],
                          dIN["s_dn_p"].ap()[j, 1, :].unsqueeze(0), "wSPD",
                          mark=f"p0.dn.sp.{j}")
                for db in range(4):
                    bsl = nb % 2
                    if s_:
                        if nb >= 2:
                            x.wait("DVE", f"p0.dn.mul.{nb - 2}")
                        x.dma(dq_qi(bsl),
                              dIN["wdnqT"].ap()[j * 128:(j + 1) * 128,
                                                db * 512:(db + 1) * 512],
                              f"wUQ{bsl}", mark=f"p0.dn.qi.{nb}")
                    if t_:
                        x.wait(f"wUQ{bsl}", f"p0.dn.qi.{nb}")
                        x.wait("wSPD", f"p0.dn.sp.{j}")
                        if nb == 0:
                            x.wait("DVE", "p0.bdnh")
                            x.wait("DVE", "p0.up.mul.31")
                        if nb >= 1:
                            x.wait("DVE", f"p0.dn.mul.{nb - 1}")
                        off = db * 512
                        x.mm(PS[6][:, :], exp2h[:, :],
                             dq_sp[0:2, off:off + 512], True, True)
                        x.mm(PS[7][:, :], lAd[:, j * 128:(j + 1) * 128],
                             dq_bd[:, off:off + 512], True, True,
                             sem="PE", mark=f"p0.dn.pe.{nb}")
                    if a_:
                        x.wait(f"wUQ{bsl}", f"p0.dn.qi.{nb}")
                        if nb >= 2:
                            x.wait("DVE", f"p0.dn.mul.{nb - 2}")
                        x.op("activation", dq_qf(bsl), dq_qi(bsl), AF.Copy,
                             bias=-7.5, sem="ACT", mark=f"p0.dn.cv.{nb}")
                    if v_:
                        x.wait("ACT", f"p0.dn.cv.{nb}")
                        x.wait("PE", f"p0.dn.pe.{nb}")
                        if nb >= 2:
                            x.wait(f"aWR{bsl}", f"p0.dn.wr.{nb - 2}")
                        else:
                            x.wait(f"aWR{bsl}", f"p0.up.wr.{30 + nb}")
                        x.op("tensor_mul", dq_qf(bsl), dq_qf(bsl), PS[6][:, :])
                        x.op("tensor_add", dq_ob(bsl), dq_qf(bsl), PS[7][:, :],
                             sem="DVE", mark=f"p0.dn.mul.{nb}")
                    if g_:
                        x.wait("DVE", f"p0.dn.mul.{nb}")
                        x.dma(wdn_c[db].ap()[j * 128:(j + 1) * 128, :],
                              dq_ob(bsl), f"aWR{bsl}", mark=f"p0.dn.wr.{nb}")
                    nb += 1
            if g_:
                x.wait("aWR0", f"p0.dn.wr.{nb - 2}")
                x.wait("aWR1", f"p0.dn.wr.{nb - 1}")
                for qd in range(4):
                    x.cc("AllGather", ALU.bypass, GROUPS8,
                         wdn_c[qd].ap().opt(), wdn_g[qd].ap().opt(),
                         f"ccD{qd}", mark=f"p0.agd.{qd}")

        def pred(ph, what):
            tbl = {
                ("p1", "xgfree"): [],
                ("p2", "xgfree"): [("PE", "p1.x1.dead"), ("DVE", "p1.g3.dne.15")],
                ("p3", "xgfree"): [("PE", "p2.M.63"), ("DVE", "p2.out2.done")],
                ("p1", "xdata"): [],
                ("p2", "xdata"): [("aYW0", "p1.g3.yw.14"), ("aYW1", "p1.g3.yw.15")],
                ("p3", "xdata"): [("aYW0", "p2.o.yw.62"), ("aYW1", "p2.o.yw.63")],
                ("p1", "ps45"): [],
                ("p2", "ps45"): [("DVE", "p1.ln3.var")],
                ("p3", "ps45"): [("DVE", "p2.ln3.var")],
                ("p1", "ps67"): [("DVE", "p0.dn.mul.31")],
                ("p2", "ps67"): [("DVE", "p1.ln3.nrm.15")],
                ("p3", "ps67"): [("DVE", "p2.ln3.nrm.15")],
            }
            return tbl[(ph, what)]

        def ln(en, x, ph, g, src, spill_x1):
            g_, t_, a_, v_ = (en == 'G'), (en == 'T'), (en == 'A'), (en == 'V')
            pre = f"{ph}.ln{g}"
            if g_:
                if g > 0:
                    x.wait("DVE", f"{ph}.ln{g - 1}.nrm.15")
                else:
                    for p in pred(ph, "xgfree") + pred(ph, "xdata"):
                        x.wait(*p)
                x.dma(R2[:, 0:16384].bitcast(F32).rearrange(
                          "p (c t) -> p c t", c=16),
                      src.ap().rearrange("(c p) t -> p c t", p=128)[
                          :, :, g * TG:(g + 1) * TG],
                      "aXG", mark=f"{pre}.x")
            for c in range(NCF):
                sl = c % 2
                if a_:
                    x.wait("aXG", f"{pre}.x")
                    if c >= 2:
                        x.wait("PE", f"{pre}.st.{c - 2}")
                    x.op("activation", sq_s(sl), xg(c), AF.Square,
                         sem="ACT", mark=f"{pre}.sq.{c}")
                if v_:
                    x.wait("aXG", f"{pre}.x")
                    if c >= 2:
                        x.wait("PE", f"{pre}.st.{c - 2}")
                    x.op("tensor_copy", xb_s(sl), xg(c),
                         sem="DVE", mark=f"{pre}.xb.{c}")
                if t_:
                    x.wait("ACT", f"{pre}.sq.{c}")
                    x.wait("DVE", f"{pre}.xb.{c}")
                    if c == 0:
                        if g > 0:
                            x.wait("DVE", f"{ph}.ln{g - 1}.var")
                        else:
                            for p in pred(ph, "ps45"):
                                x.wait(*p)
                    x.mm(PS[4][0:1, :], onec[:, :], xb_s(sl), c == 0, c == NCF - 1)
                    x.mm(PS[5][0:1, :], onec[:, :], sq_s(sl), c == 0, c == NCF - 1,
                         sem="PE", mark=f"{pre}.st.{c}")
            if v_:
                x.wait("PE", f"{pre}.st.15")
                x.op("tensor_scalar_mul", row_m, PS[4][0:1, :], 1.0 / F)
                x.op("tensor_scalar_mul", row_v, PS[5][0:1, :], 1.0 / F)
                x.op("tensor_mul", row_r, row_m, row_m)
                x.op("tensor_sub", row_v, row_v, row_r,
                     sem="DVE", mark=f"{pre}.var")
            if a_:
                x.wait("DVE", f"{pre}.var")
                x.op("activation", row_v, row_v, AF.Sqrt, bias=EPS,
                     sem="ACT", mark=f"{pre}.sqrt")
            if v_:
                x.wait("ACT", f"{pre}.sqrt")
                x.op("reciprocal", row_r, row_v)
                x.op("tensor_mul", row_mr, row_m, row_r,
                     sem="DVE", mark=f"{pre}.rows")
            if t_:
                x.wait("DVE", f"{pre}.rows")
                if g > 0:
                    x.wait("DVE", f"{ph}.ln{g - 1}.nrm.15")
                else:
                    for p in pred(ph, "ps67"):
                        x.wait(*p)
                x.mm(PS[6][:, :], oner[:, :], row_r, True, True)
                x.mm(PS[7][:, :], oner[:, :], row_mr, True, True,
                     sem="PE", mark=f"{pre}.pl")
            for c in range(NCF):
                if v_:
                    x.wait("PE", f"{pre}.pl")
                    if spill_x1 and c >= 2:
                        x.wait(f"aX1W{c % 2}", f"{pre}.x1w.{c - 2}")
                    elif spill_x1 and g > 0:
                        x.wait(f"aX1W{c % 2}", f"{ph}.ln{g - 1}.x1w.{14 + c}")
                    if (not spill_x1) and g == 0 and c == 0 and ph == "p3":
                        x.wait("aX1W0", "p2.ln3.x1w.14")
                        x.wait("aX1W1", "p2.ln3.x1w.15")
                    x.op("tensor_mul", xg(c), xg(c), PS[6][:, :])
                    dst = x1st(c % 2) if spill_x1 else x1g(c)
                    x.op("tensor_sub", dst, xg(c), PS[7][:, :],
                         sem="DVE", mark=f"{pre}.nrm.{c}")
                if spill_x1 and g_:
                    x.wait("DVE", f"{pre}.nrm.{c}")
                    x.dma(x1_d.ap()[c * 128:(c + 1) * 128, g * TG:(g + 1) * TG],
                          x1st(c % 2), f"aX1W{c % 2}", mark=f"{pre}.x1w.{c}")

        def ffn(en, x, ph, src, dst):
            s_, g_, t_, a_, v_ = (en == 'S'), (en == 'G'), (en == 'T'), (en == 'A'), (en == 'V')
            is_p1 = (ph == "p1")
            for g in range(NG):
                ln(en, x, ph, g, src, spill_x1=False)
                for hr in range(16):
                    sl = hr % 2
                    if s_:
                        if hr >= 2:
                            x.wait("PE", f"{ph}.g{g}.up.r{hr - 2}.done")
                        elif g > 0:
                            x.wait("PE", f"{ph}.g{g - 1}.up.r{14 + hr}.done")
                        elif is_p1:
                            x.wait("aWR0", "p0.dn.wr.30")
                            x.wait("aWR1", "p0.dn.wr.31")
                        else:
                            x.wait("PE", "p2.M.63")
                        x.wait(f"ccU{hr // 2}", f"p0.agu.{hr // 2}")
                        for c in range(NCF):
                            mk = f"{ph}.g{g}.upl.{hr}" if c == NCF - 1 else None
                            x.dma(upb(sl)[:, c * 512:(c + 1) * 512],
                                  wup_g[hr // 2].ap()[
                                      c * 128:(c + 1) * 128,
                                      (hr % 2) * 512:(hr % 2) * 512 + 512],
                                  f"wUP{sl}", mark=mk)
                    if t_:
                        x.wait(f"wUP{hr % 2}", f"{ph}.g{g}.upl.{hr}")
                        if hr == 0:
                            x.wait("DVE", f"{ph}.ln{g}.nrm.15")
                        for ht in range(4):
                            htg = hr * 4 + ht
                            bank = htg % 4
                            if htg >= 4:
                                x.wait("ACT", f"{ph}.g{g}.upe.{htg - 4}")
                            elif g > 0:
                                x.wait("DVE", f"{ph}.g{g - 1}.dne.{12 + htg}")
                            elif ph == "p3":
                                x.wait("DVE", "p2.out2.done")
                            for c in range(NCF):
                                last = (c == NCF - 1)
                                mk = None
                                if last:
                                    mk = (f"{ph}.g{g}.up.r{hr}.done" if ht == 3
                                          else f"{ph}.g{g}.up.{htg}")
                                x.mm(PS[bank][:, :],
                                     upb(sl)[:, c * 512 + ht * 128:
                                             c * 512 + ht * 128 + 128],
                                     x1g(c), c == 0, last,
                                     sem="PE" if last else None, mark=mk)
                    if a_:
                        for ht in range(4):
                            htg = hr * 4 + ht
                            bank = htg % 4
                            mk = (f"{ph}.g{g}.up.r{hr}.done" if ht == 3
                                  else f"{ph}.g{g}.up.{htg}")
                            x.wait("PE", mk)
                            if g == 0 and htg == 0:
                                if is_p1:
                                    x.wait("aCVW0", "p0.cvw.6")
                                    x.wait("aCVW1", "p0.cvw.7")
                                else:
                                    x.wait("PE", "p2.o.15.done")
                            x.op("activation", bf(R1, htg * 512, 512),
                                 PS[bank][:, :], AF.Relu,
                                 bias=bupt[:, htg:htg + 1],
                                 sem="ACT", mark=f"{ph}.g{g}.upe.{htg}")
                for ss in range(16):
                    sl = ss % 2
                    qd, qc = ss // 4, ss % 4
                    if s_:
                        if ss >= 2:
                            x.wait("PE", f"{ph}.g{g}.dn.{ss - 2}")
                        elif g > 0:
                            x.wait("PE", f"{ph}.g{g - 1}.dn.{14 + ss}")
                        elif is_p1:
                            x.wait("aWR0", "p0.dn.wr.30")
                            x.wait("aWR1", "p0.dn.wr.31")
                        else:
                            x.wait("PE", "p2.M.63")
                        x.wait(f"ccD{qd}", f"p0.agd.{qd}")
                        x.dma(dnb(sl).rearrange("p (h c) -> p h c", h=64),
                              wdn_g[qd].ap().rearrange("(h p) c -> p h c", p=128)[
                                  :, :, qc * 128:(qc + 1) * 128],
                              f"wDN{sl}", mark=f"{ph}.g{g}.dnl.{ss}")
                    if g_:
                        if ss >= 2:
                            x.wait(f"aYW{sl}", f"{ph}.g{g}.yw.{ss - 2}")
                        elif g > 0:
                            x.wait(f"aYW{sl}", f"{ph}.g{g - 1}.yw.{14 + ss}")
                        elif ph == "p3":
                            x.wait(f"aYW{sl}", f"p2.o.yw.{62 + ss}")
                        x.dma(xres[sl][:, :],
                              src.ap()[ss * 128:(ss + 1) * 128,
                                       g * TG:(g + 1) * TG],
                              f"aXR{sl}", mark=f"{ph}.g{g}.xr.{ss}")
                    if t_:
                        bank = ss % 4
                        x.wait(f"wDN{ss % 2}", f"{ph}.g{g}.dnl.{ss}")
                        if ss == 0:
                            x.wait("ACT", f"{ph}.g{g}.upe.63")
                        if ss >= 4:
                            x.wait("DVE", f"{ph}.g{g}.dne.{ss - 4}")
                        elif g > 0:
                            x.wait("DVE", f"{ph}.g{g - 1}.dne.{12 + ss}")
                        for h in range(64):
                            last = (h == 63)
                            x.mm(PS[bank][:, :],
                                 dnb(sl)[:, h * 128:(h + 1) * 128],
                                 bf(R1, h * 512, 512), h == 0, last,
                                 sem="PE" if last else None,
                                 mark=f"{ph}.g{g}.dn.{ss}" if last else None)
                    if v_:
                        bank = ss % 4
                        x.wait("PE", f"{ph}.g{g}.dn.{ss}")
                        x.wait(f"aXR{ss % 2}", f"{ph}.g{g}.xr.{ss}")
                        if ss >= 2:
                            x.wait(f"aYW{ss % 2}", f"{ph}.g{g}.yw.{ss - 2}")
                        elif g > 0:
                            x.wait(f"aYW{ss % 2}", f"{ph}.g{g - 1}.yw.{14 + ss}")
                        x.op("scalar_tensor_tensor", xres[sl][:, :],
                             PS[bank][:, :], bdnh[:, ss:ss + 1], xres[sl][:, :],
                             ALU.add, ALU.add,
                             sem="DVE", mark=f"{ph}.g{g}.dne.{ss}")
                    if g_:
                        x.wait("DVE", f"{ph}.g{g}.dne.{ss}")
                        x.dma(dst.ap()[ss * 128:(ss + 1) * 128,
                                       g * TG:(g + 1) * TG],
                              xres[sl][:, :], f"aYW{sl}", mark=f"{ph}.g{g}.yw.{ss}")
        def attn(en, x):
            s_, g_, t_, a_, v_ = (en == 'S'), (en == 'G'), (en == 'T'), (en == 'A'), (en == 'V')
            for g in range(NG):
                ln(en, x, "p2", g, x_a, spill_x1=True)
            for wi, arena in ((0, R1), (1, R3)):
                wn = "q" if wi == 0 else "k"
                if s_:
                    if wi == 0:
                        x.wait("DVE", "p2.ln3.nrm.15")
                        x.wait("aX1W0", "p2.ln3.x1w.14")
                        x.wait("aX1W1", "p2.ln3.x1w.15")
                    else:
                        x.wait("PE", "p2.q.15.done")
                    for c in range(NCF):
                        x.dma(bf(R2, c * 2048, 2048),
                              wqkvo_g.ap()[wrow(wi, c):wrow(wi, c) + 128, :],
                              "wW", mark=f"p2.{wn}.wl" if c == NCF - 1 else None)
                if g_:
                    for st_ in range(16):
                        sl = st_ % 2
                        if st_ >= 2:
                            x.wait("PE", f"p2.{wn}.{st_ - 2}.done")
                        elif wn == "k":
                            x.wait("PE", f"p2.q.{14 + st_}.done")
                        x.dma(x1tb[sl][:, :].rearrange("p (c s) -> p c s", c=16),
                              x1_d.ap().rearrange("(c p) s -> p c s", p=128)[
                                  :, :, st_ * 128:(st_ + 1) * 128],
                              f"aX1R{sl}", mark=f"p2.{wn}.x1.{st_}")
                if t_:
                    x.wait("wW", f"p2.{wn}.wl")
                    if wi == 0:
                        x.wait("ACT", "p1.g3.upe.63")
                        x.wait("DVE", "p1.g3.dne.15")
                    for st_ in range(16):
                        sl = st_ % 2
                        x.wait(f"aX1R{sl}", f"p2.{wn}.x1.{st_}")
                        for eg in range(4):
                            n = st_ * 4 + eg
                            bank = n % 4
                            if n >= 4:
                                x.wait("ACT", f"p2.{wn}.ev.{n - 4}")
                            elif wn == "k":
                                x.wait("ACT", f"p2.q.ev.{60 + n}")
                            for c in range(NCF):
                                last = (c == NCF - 1)
                                mk = None
                                if last:
                                    mk = (f"p2.{wn}.{st_}.done" if eg == 3
                                          else f"p2.{wn}.{st_}.{eg}")
                                x.mm(PS[bank][:, :],
                                     x1tb[sl][:, c * 128:(c + 1) * 128],
                                     bf(R2, c * 2048 + eg * 512, 512),
                                     c == 0, last,
                                     sem="PE" if last else None, mark=mk)
                if a_:
                    for st_ in range(16):
                        for eg in range(4):
                            n = st_ * 4 + eg
                            bank = n % 4
                            mk = (f"p2.{wn}.{st_}.done" if eg == 3
                                  else f"p2.{wn}.{st_}.{eg}")
                            x.wait("PE", mk)
                            if wi == 1 and n == 0:
                                x.wait("PE", "p1.g3.dn.15")
                            x.op("activation",
                                 arena[:, st_ * 2048 + eg * 512:
                                       st_ * 2048 + eg * 512 + 512],
                                 PS[bank][:, :], AF.Copy,
                                 sem="ACT", mark=f"p2.{wn}.ev.{n}")
            for et in range(16):
                for fg in range(4):
                    n = et * 4 + fg
                    bank = n % 4
                    if t_:
                        if n == 0:
                            x.wait("ACT", "p2.k.ev.63")
                        if n >= 4:
                            x.wait("ACT", f"p2.sc.ev.{n - 4}")
                        for st_ in range(16):
                            last = (st_ == 15)
                            x.mm(PS[bank][:, :],
                                 R1[:, st_ * 2048 + et * 128:
                                    st_ * 2048 + et * 128 + 128],
                                 R3[:, st_ * 2048 + fg * 512:
                                    st_ * 2048 + fg * 512 + 512],
                                 st_ == 0, last,
                                 sem="PE" if last else None,
                                 mark=f"p2.sc.{n}" if last else None)
                    if a_:
                        x.wait("PE", f"p2.sc.{n}")
                        if et >= 1 and fg == 0:
                            x.wait("aSC", f"p2.scw.{et - 1}")
                        x.op("activation", scst[:, fg * 512:(fg + 1) * 512],
                             PS[bank][:, :], AF.Copy,
                             sem="ACT", mark=f"p2.sc.ev.{n}")
                if g_:
                    x.wait("ACT", f"p2.sc.ev.{et * 4 + 3}")
                    x.dma(sc_p.ap()[et * 128:(et + 1) * 128, :], scst,
                          "aSC", mark=f"p2.scw.{et}")
                    if et % 4 == 3:
                        q = et // 4
                        x.wait("aSC", f"p2.scw.{et}")
                        x.cc("AllReduce", ALU.add, GROUPS4,
                             sc_p.ap()[q * 512:(q + 1) * 512, :].opt(),
                             sc_r.ap()[q * 512:(q + 1) * 512, :].opt(),
                             f"ccS{q}", mark=f"p2.ar.{q}")
            for half in range(2):
                if s_:
                    if half == 0:
                        x.wait("PE", "p2.k.15.done")
                    else:
                        x.wait("PE", "p2.v.c03")
                    for c in range(NCF):
                        x.dma(wvh[:, c * 1024:(c + 1) * 1024],
                              wqkvo_g.ap()[wrow(2, c):wrow(2, c) + 128,
                                           half * 1024:half * 1024 + 1024],
                              "wW", mark=f"p2.v.wl.{half}" if c == NCF - 1 else None)
                for sg in range(NG):
                    vsl = sg % 2
                    if g_:
                        if half == 0 and sg == 0:
                            x.wait("PE", "p2.sc.63")
                        nprev = half * 4 + sg - 2
                        if nprev >= 0:
                            hp, sp_ = nprev // 4, nprev % 4
                            x.wait("PE", f"p2.v.c{hp}{sp_}")
                        x.dma(x1v(vsl).rearrange("p (c s) -> p c s", c=16),
                              x1_d.ap().rearrange("(c p) s -> p c s", p=128)[
                                  :, :, sg * TG:(sg + 1) * TG],
                              f"aXV{vsl}", mark=f"p2.v.x1.{half}.{sg}")
                    if t_:
                        x.wait("wW", f"p2.v.wl.{half}")
                        x.wait(f"aXV{vsl}", f"p2.v.x1.{half}.{sg}")
                        for fi in range(8):
                            ft = half * 8 + fi
                            n = half * 32 + sg * 8 + fi
                            bank = n % 4
                            if n >= 4:
                                x.wait("ACT", f"p2.v.ev.{n - 4}")
                            elif n < 4:
                                x.wait("ACT", f"p2.sc.ev.{60 + n}")
                            for c in range(NCF):
                                last = (c == NCF - 1)
                                mk = None
                                if last:
                                    mk = (f"p2.v.c{half}{sg}" if fi == 7
                                          else f"p2.v.n.{n}")
                                x.mm(PS[bank][:, :],
                                     wvh[:, c * 1024 + fi * 128:
                                         c * 1024 + fi * 128 + 128],
                                     x1v(vsl)[:, c * 512:(c + 1) * 512],
                                     c == 0, last,
                                     sem="PE" if last else None, mark=mk)
                    if a_:
                        for fi in range(8):
                            ft = half * 8 + fi
                            n = half * 32 + sg * 8 + fi
                            bank = n % 4
                            mk = (f"p2.v.c{half}{sg}" if fi == 7
                                  else f"p2.v.n.{n}")
                            x.wait("PE", mk)
                            if n == 0:
                                x.wait("PE", "p2.sc.63")
                            x.op("activation",
                                 R1[:, ft * 2048 + sg * 512:
                                    ft * 2048 + sg * 512 + 512],
                                 PS[bank][:, :], AF.Copy,
                                 sem="ACT", mark=f"p2.v.ev.{n}")
            for et in range(16):
                q = et // 4
                sl = et % 2
                if g_:
                    x.wait(f"ccS{q}", f"p2.ar.{q}")
                    if et >= 2:
                        x.wait("ACT", f"p2.sm.exp.{et - 2}")
                    x.dma(smx(sl), sc_r.ap()[et * 128:(et + 1) * 128, :],
                          f"aSM{sl}", mark=f"p2.sm.l.{et}")
                if v_:
                    x.wait(f"aSM{sl}", f"p2.sm.l.{et}")
                    if et > 0:
                        x.wait("ACT", f"p2.sm.exp.{et - 1}")
                    x.op("reduce_max", nmax[:, :], smx(sl), axis=AX.X,
                         negate=True, sem="DVE", mark=f"p2.sm.mx.{et}")
                if a_:
                    x.wait("DVE", f"p2.sm.mx.{et}")
                    if et == 0:
                        x.wait("PE", "p2.sc.63")
                    if et < 8:
                        x.wait("PE", "p2.v.c13")
                    x.op("activation", wsm(et), smx(sl), AF.Exp,
                         bias=nmax[:, :], accum_out=rs_c[:, :],
                         sem="ACT", mark=f"p2.sm.exp.{et}")
                if v_:
                    x.wait("ACT", f"p2.sm.exp.{et}")
                    x.op("reciprocal", rr_all[:, et:et + 1], rs_c[:, :],
                         sem="DVE", mark=f"p2.sm.rr.{et}")
            for e in list(range(8, 16)) + list(range(8)):
                if s_:
                    if e == 8:
                        x.wait("PE", "p2.v.c13")
                    if e == 0:
                        x.wait("ACT", "p2.sm.exp.15")
                        x.wait("aSC", "p2.scw.15")
                    x.dma(wo_sb(e),
                          wqkvo_g.ap()[wrow(3, e):wrow(3, e) + 128, :],
                          "wW", mark=f"p2.wo.l.{e}")
                if v_:
                    x.wait("wW", "p2.wo.l.7")
                    x.wait("DVE", "p2.sm.rr.15")
                    x.op("tensor_scalar_mul", wo_sb(e), wo_sb(e),
                         rr_all[:, e:e + 1],
                         sem="DVE", mark=f"p2.wo.f.{e}")
            for eo in range(4):
                for ft in range(16):
                    n = eo * 16 + ft
                    bank = n % 4
                    if t_:
                        if n == 0:
                            x.wait("ACT", "p2.sm.exp.15")
                            x.wait("DVE", "p2.wo.f.7")
                        if n >= 4:
                            x.wait("ACT", f"p2.M.ev.{n - 4}")
                        for e in range(16):
                            last = (e == 15)
                            x.mm(PS[bank][:, :],
                                 R3[:, e * 2048 + ft * 128:
                                    e * 2048 + ft * 128 + 128],
                                 wo_sb(e)[:, eo * 512:(eo + 1) * 512],
                                 e == 0, last,
                                 sem="PE" if last else None,
                                 mark=f"p2.M.{n}" if last else None)
                    if a_:
                        x.wait("PE", f"p2.M.{n}")
                        if n >= 2:
                            x.wait(f"aMW{n % 2}", f"p2.M.w.{n - 2}")
                        x.op("activation", mstg[n % 2][:, :], PS[bank][:, :],
                             AF.Copy, sem="ACT", mark=f"p2.M.ev.{n}")
                    if g_:
                        x.wait("ACT", f"p2.M.ev.{n}")
                        x.dma(M_d.ap()[ft * 128:(ft + 1) * 128,
                                       eo * 512:(eo + 1) * 512],
                              mstg[n % 2][:, :], f"aMW{n % 2}", mark=f"p2.M.w.{n}")
            for et in range(16):
                sl = et % 2
                if s_:
                    if et >= 2:
                        x.wait("PE", f"p2.o.{et - 2}.done")
                    else:
                        x.wait("PE", "p2.k.15.done")
                    x.wait("aMW0", f"p2.M.w.{(et // 4) * 16 + 14}")
                    x.wait("aMW1", f"p2.M.w.{(et // 4) * 16 + 15}")
                    x.dma(x1tb[sl][:, :].rearrange("p (c e) -> p c e", c=16),
                          M_d.ap().rearrange("(c p) e -> p c e", p=128)[
                              :, :, et * 128:(et + 1) * 128],
                          f"wM{sl}", mark=f"p2.o.ml.{et}")
                for sg in range(4):
                    n = et * 4 + sg
                    bank = n % 4
                    amark = "p2.out2.done" if n == 63 else f"p2.o.add.{n}"
                    if g_:
                        if n >= 2:
                            x.wait(f"aYW{n % 2}", f"p2.o.yw.{n - 2}")
                        else:
                            x.wait("aMW0", "p2.M.w.62")
                            x.wait("aMW1", "p2.M.w.63")
                            x.wait(f"aYW{n % 2}", f"p1.g3.yw.{14 + n}")
                        x.dma(xres[n % 2][:, :],
                              x_a.ap()[et * 128:(et + 1) * 128,
                                       sg * TG:(sg + 1) * TG],
                              f"aXR{n % 2}", mark=f"p2.o.xr.{n}")
                    if t_:
                        if n == 0:
                            x.wait("ACT", "p2.M.ev.63")
                        x.wait(f"wM{sl}", f"p2.o.ml.{et}")
                        if n >= 4:
                            x.wait("DVE", f"p2.o.add.{n - 4}")
                        for c in range(NCF):
                            last = (c == NCF - 1)
                            mk = None
                            if last:
                                mk = (f"p2.o.{et}.done" if sg == 3
                                      else f"p2.o.n.{n}")
                            x.mm(PS[bank][:, :],
                                 x1tb[sl][:, c * 128:(c + 1) * 128],
                                 R1[:, c * 2048 + sg * 512:
                                    c * 2048 + sg * 512 + 512],
                                 c == 0, last,
                                 sem="PE" if last else None, mark=mk)
                    if v_:
                        mk = f"p2.o.{et}.done" if sg == 3 else f"p2.o.n.{n}"
                        x.wait("PE", mk)
                        x.wait(f"aXR{n % 2}", f"p2.o.xr.{n}")
                        if n >= 2:
                            x.wait(f"aYW{n % 2}", f"p2.o.yw.{n - 2}")
                        x.op("tensor_add", xres[n % 2][:, :], xres[n % 2][:, :],
                             PS[bank][:, :], sem="DVE", mark=amark)
                    if g_:
                        x.wait("DVE", amark)
                        x.dma(x_b.ap()[et * 128:(et + 1) * 128,
                                       sg * TG:(sg + 1) * TG],
                              xres[n % 2][:, :], f"aYW{n % 2}", mark=f"p2.o.yw.{n}")

        def emit_all(en, eng):
            x = E(ctx, eng, en)
            p0(en, x)
            if n_phases >= 1:
                ffn(en, x, "p1", dIN["x_in"], x_a if n_phases > 1 else y_out)
                if ctx.dry and en == 'T':
                    ctx.plan["p1.x1.dead"] = ctx.plan["p1.g3.up.r15.done"]
            if n_phases >= 2:
                attn(en, x)
                if n_phases == 2 and en == 'G':
                    x.wait("aYW0", "p2.o.yw.62")
                    x.wait("aYW1", "p2.o.yw.63")
                    x.dma(y_out.ap(), x_b.ap(), "aYW0", mark="fin.copy")
            if n_phases >= 3:
                ffn(en, x, "p3", x_b, y_out)
            if en == 'G':
                for s in W_SEMS + A_SEMS:
                    x.wait_total(s)

        ctx.dry = True
        for en in ("S", "G", "T", "A", "V"):
            emit_all(en, None)
        ctx.totals = dict(ctx.sems)

        ctx.dry = False
        with nc.Block() as block:
            @block.sync
            def _(sync):
                ctx.sems = {}
                emit_all("S", sync)

            @block.gpsimd
            def _(gpsimd):
                ctx.sems = {}
                emit_all("G", gpsimd)

            @block.tensor
            def _(tensor):
                ctx.sems = {}
                emit_all("T", tensor)

            @block.scalar
            def _(scalar):
                ctx.sems = {}
                emit_all("A", scalar)

            @block.vector
            def _(vector):
                ctx.sems = {}
                emit_all("V", vector)

    return nc


def make_inputs(L0_attn_x, L0_attn_Wq, L0_attn_Wk, L0_attn_Wv, L0_attn_Wo,
                L0_ffn_w_up_q, L0_ffn_w_up_scale, L0_ffn_b_up,
                L0_ffn_w_up_lora_a, L0_ffn_w_up_lora_b,
                L0_ffn_w_down_q, L0_ffn_w_down_scale, L0_ffn_b_down,
                L0_ffn_w_down_lora_a, L0_ffn_w_down_lora_b):
    C = np.ascontiguousarray
    x = np.asarray(L0_attn_x)
    wupqT = np.asarray(L0_ffn_w_up_q).T
    s_upT = np.asarray(L0_ffn_w_up_scale).T
    wdnqT = np.asarray(L0_ffn_w_down_q).T
    s_dnT = np.asarray(L0_ffn_w_down_scale).T
    Wq = np.asarray(L0_attn_Wq); Wk = np.asarray(L0_attn_Wk)
    Wv = np.asarray(L0_attn_Wv); Wo = np.asarray(L0_attn_Wo)
    b_up = np.asarray(L0_ffn_b_up); b_dn = np.asarray(L0_ffn_b_down)
    AupT = np.asarray(L0_ffn_w_up_lora_a).T
    Bup = np.asarray(L0_ffn_w_up_lora_b)
    AdnT = np.asarray(L0_ffn_w_down_lora_a).T
    Bdn = np.asarray(L0_ffn_w_down_lora_b)
    b_up_t = C(b_up.reshape(64, 128).T).astype(np.float32)
    b_dn_t = C(b_dn.reshape(16, 128).T).astype(np.float32)
    ones_col = np.ones((128, 1), BF)
    ones_row = np.ones((1, 128), np.float32)
    exp2 = np.zeros((2, 128), np.float32)
    exp2[0, :64] = 1.0
    exp2[1, 64:] = 1.0
    exp2h = exp2 * 0.5
    constv_np = np.zeros((128, 2), np.float32)
    constv_np[:, 0] = -7.5
    constv_np[:, 1] = EPS

    in_maps = []
    for c in range(8):
        b, sl = c // 4, c % 4
        x_fm = C(x[b, sl * T:(sl + 1) * T, :].T).astype(np.float32)
        sp = np.zeros((2, 2, 4, 2048), np.float32)
        for ct in range(2):
            g0 = 4 * c + 2 * ct
            for q in range(4):
                sp[ct, 0, q, :] = s_upT[g0, q * 2048:(q + 1) * 2048]
                sp[ct, 1, q, :] = s_upT[g0 + 1, q * 2048:(q + 1) * 2048]
        sdp = np.zeros((8, 2, F), np.float32)
        for j in range(8):
            g0 = (1024 * c + 128 * j) // 64
            sdp[j, 0, :] = s_dnT[g0, :]
            sdp[j, 1, :] = s_dnT[g0 + 1, :]
        wqkvo_sl = np.concatenate([
            C(Wq[256 * c:256 * (c + 1), :]),
            C(Wk[256 * c:256 * (c + 1), :]),
            C(Wv[256 * c:256 * (c + 1), :]),
            C(Wo[256 * c:256 * (c + 1), :])], axis=0).astype(np.float32)
        in_maps.append({
            "x_in": x_fm,
            "wupqT": C(wupqT[256 * c:256 * (c + 1), :]).astype(np.int32),
            "wdnqT": C(wdnqT[1024 * c:1024 * (c + 1), :]).astype(np.int32),
            "s_up_p": sp, "s_dn_p": sdp,
            "wqkvo_sl": wqkvo_sl,
            "b_up_t": b_up_t, "b_dn_t": b_dn_t,
            "AupT_sl": C(AupT[:, 256 * c:256 * (c + 1)]).astype(np.float32),
            "Bup": C(Bup).astype(np.float32),
            "AdnT_sl": C(AdnT[:, 1024 * c:1024 * (c + 1)]).astype(np.float32),
            "Bdn": C(Bdn).astype(np.float32),
            "ones_col": ones_col, "ones_row": ones_row,
            "exp2": exp2, "exp2h": exp2h,
            "constv": constv_np,
        })
    return in_maps


N_PHASES = 3


def kernel(**inputs):
    in_maps = make_inputs(**inputs)
    nc = build_nc(N_PHASES)
    res = run_bass_kernel_spmd(nc, in_maps, core_ids=list(range(8)))
    out = np.zeros((2, 8192, 2048), np.float32)
    for c in range(8):
        b, sl = c // 4, c % 4
        out[b, sl * T:(sl + 1) * T, :] = res.results[c]["y_out"].T
    return out


# version 02
# speedup vs baseline: 1.0448x; 1.0448x over previous
import contextlib
import numpy as np
import ml_dtypes
import concourse.bass as bass
import concourse.mybir as mybir
from concourse.bass_utils import run_bass_kernel_spmd

F32 = mybir.dt.float32
BF16 = mybir.dt.bfloat16
I32 = mybir.dt.int32
AF = mybir.ActivationFunctionType
ALU = mybir.AluOpType
AX = mybir.AxisListType
BF = ml_dtypes.bfloat16

F = 2048
T = 2048
H = 8192
NCF = 16
NG = 4
TG = 512
EPS = 1e-5


class Ctx:
    def __init__(self):
        self.sems = {}
        self.plan = {}
        self.dry = True
        self.S = {}
        self.totals = {}

    def inc(self, sem, n, mark=None):
        self.sems[sem] = self.sems.get(sem, 0) + n
        if mark is not None:
            if self.dry:
                assert mark not in self.plan, f"dup mark {mark}"
                self.plan[mark] = self.sems[sem]
            else:
                assert self.plan[mark] == self.sems[sem], (
                    f"plan mismatch {mark}: plan={self.plan[mark]} "
                    f"emit={self.sems[sem]}")
        return self.sems[sem]


class E:

    AUTO = {"V": "DVE", "A": "ACT"}

    def __init__(self, ctx, eng, en=None):
        self.ctx = ctx
        self.eng = eng
        self.auto = self.AUTO.get(en)

    def wait(self, sem, mark):
        if self.ctx.dry:
            assert isinstance(mark, str)
            return
        v = self.ctx.plan[mark]
        if v > 0:
            self.eng.wait_ge(self.ctx.S[sem], v)

    def wait_total(self, sem):
        if self.ctx.dry:
            return
        v = self.ctx.totals.get(sem, 0)
        if v > 0:
            self.eng.wait_ge(self.ctx.S[sem], v)

    def dma(self, out, in_, sem, mark=None, **kw):
        if self.ctx.dry:
            self.ctx.inc(sem, 16, mark)
            return
        ins = self.eng.dma_start(out=out, in_=in_, **kw)
        ins.then_inc(self.ctx.S[sem], 16)
        self.ctx.inc(sem, 16, mark)

    def cc(self, kind, op, groups, in_ap, out_ap, sem, mark=None):
        if self.ctx.dry:
            self.ctx.inc(sem, 1, mark)
            return
        ins = self.eng.collective_compute(
            kind, op, replica_groups=groups, ins=[in_ap], outs=[out_ap])
        ins.then_inc(self.ctx.S[sem])
        self.ctx.inc(sem, 1, mark)

    def op(self, fn, *args, sem=None, mark=None, **kw):
        if self.auto is not None:
            prev = self.ctx.sems.get(self.auto, 0)
            if not self.ctx.dry and prev > 0:
                self.eng.wait_ge(self.ctx.S[self.auto], prev)
            if sem is None:
                sem = self.auto
            else:
                assert sem == self.auto, (sem, self.auto)
        if self.ctx.dry:
            if sem is not None:
                self.ctx.inc(sem, 1, mark)
            return
        ins = getattr(self.eng, fn)(*args, **kw)
        if sem is not None:
            ins.then_inc(self.ctx.S[sem])
            self.ctx.inc(sem, 1, mark)

    def mm(self, out, lhsT, rhs, start, stop, sem=None, mark=None):
        self.op("matmul", out, lhsT, rhs, start=start, stop=stop,
                sem=sem, mark=mark)


def build_nc(n_phases=3):
    nc = bass.Bass("TRN2", target_bir_lowering=False, debug=False)

    dIN = {}
    def din(name, shape, dt):
        dIN[name] = nc.dram_tensor(name, shape, dt, kind="ExternalInput")
        return dIN[name]

    din("x_in", [F, T], F32)
    din("wupqT", [256, H], I32)
    din("wdnqT", [1024, F], I32)
    din("s_up_p", [2, 4, 128, 2048], F32)
    din("s_dn_p", [8, 128, F], F32)
    din("wqkvo_sl", [1024, F], F32)
    din("b_up_t", [128, 64], F32)
    din("b_dn_t", [128, 16], F32)
    din("AupT_sl", [16, 256], F32)
    din("Bup", [16, H], F32)
    din("AdnT_sl", [16, 1024], F32)
    din("Bdn", [16, F], F32)
    din("ones_col", [128, 1], BF16)
    din("ones_row", [1, 128], F32)
    din("exp2", [2, 128], F32)
    din("exp2h", [2, 128], F32)
    din("constv", [128, 4], F32)

    y_out = nc.dram_tensor("y_out", [F, T], F32, kind="ExternalOutput")

    wup_c = [nc.dram_tensor(f"wup_c{r}", [256, 1024], BF16) for r in range(8)]
    wup_g = [nc.dram_tensor(f"wup_g{r}", [F, 1024], BF16) for r in range(8)]
    wdn_c = [nc.dram_tensor(f"wdn_c{r}", [1024, 512], BF16) for r in range(4)]
    wdn_g = [nc.dram_tensor(f"wdn_g{r}", [H, 512], BF16) for r in range(4)]
    wqkvo_c = nc.dram_tensor("wqkvo_c", [1024, F], BF16)
    wqkvo_g = nc.dram_tensor("wqkvo_g", [H, F], BF16)
    x_a = nc.dram_tensor("x_a", [F, T], F32)
    x_b = nc.dram_tensor("x_b", [F, T], F32)
    x1_d = nc.dram_tensor("x1_d", [F, T], BF16)
    sc_p = nc.dram_tensor("sc_p", [F, F], F32)
    sc_r = nc.dram_tensor("sc_r", [F, F], F32)
    M_d = nc.dram_tensor("M_d", [F, F], BF16)

    ctx = Ctx()
    W_SEMS = (["wCV0", "wCV1", "wSP0", "wSP1", "wSPD", "wUQ0", "wUQ1",
               "wUP0", "wUP1", "wDN0", "wDN1", "wW", "wM0", "wM1"])
    A_SEMS = (["aCON", "aCVW0", "aCVW1", "aWR0", "aWR1", "aXG", "aXR0", "aXR1",
               "aX1W0", "aX1W1", "aX1R0", "aX1R1", "aXV0", "aXV1", "aSC",
               "aSM0", "aSM1", "aMW0", "aMW1", "aYW0", "aYW1"])
    sem_names = (W_SEMS + A_SEMS + ["PE", "ACT", "DVE"]
                 + [f"ccU{r}" for r in range(8)]
                 + [f"ccD{r}" for r in range(4)]
                 + ["ccQ"] + [f"ccS{r}" for r in range(4)])

    with contextlib.ExitStack() as st:
        R1 = st.enter_context(nc.sbuf_tensor("R1", [128, 32768], BF16))
        R2 = st.enter_context(nc.sbuf_tensor("R2", [128, 32768], BF16))
        R3 = st.enter_context(nc.sbuf_tensor("R3", [128, 32768], BF16))
        onec = st.enter_context(nc.sbuf_tensor("onec", [128, 1], BF16))
        oner = st.enter_context(nc.sbuf_tensor("oner", [1, 128], F32))
        exp2 = st.enter_context(nc.sbuf_tensor("exp2_sb", [2, 128], F32))
        exp2h = st.enter_context(nc.sbuf_tensor("exp2h_sb", [2, 128], F32))
        bupt = st.enter_context(nc.sbuf_tensor("bupt", [128, 64], F32))
        bdnt = st.enter_context(nc.sbuf_tensor("bdnt", [128, 16], F32))
        bdnh = st.enter_context(nc.sbuf_tensor("bdnh", [128, 16], F32))
        xres = [st.enter_context(nc.sbuf_tensor(f"xres{i}", [128, 512], F32))
                for i in range(2)]
        x1tb = [st.enter_context(nc.sbuf_tensor(f"x1tb{i}", [128, 2048], BF16))
                for i in range(2)]
        constv = st.enter_context(nc.sbuf_tensor("constv_sb", [128, 4], F32))
        nmax = st.enter_context(nc.sbuf_tensor("nmax", [128, 1], F32))
        mstg = [None, None]
        rs_c = st.enter_context(nc.sbuf_tensor("rs_c", [128, 1], F32))
        rr_all = st.enter_context(nc.sbuf_tensor("rr_all", [128, 16], F32))
        PS = [st.enter_context(nc.psum_tensor(f"ps{i}", [128, 512], F32))
              for i in range(8)]
        for s in sem_names:
            ctx.S[s] = st.enter_context(nc.semaphore(s))

        def bf(a, col, n):
            return a[:, col:col + n]

        def f32v(a, col, n):
            return a[:, col:col + 2 * n].bitcast(F32)

        def i32v(a, col, n):
            return a[:, col:col + 2 * n].bitcast(I32)

        mstg[0] = xres[0][:, 0:256].bitcast(BF16)
        mstg[1] = xres[1][:, 0:256].bitcast(BF16)

        nc.const_aps.register(F32, -7.5, constv[:, 0:1])
        nc.const_aps.register(F32, EPS, constv[:, 1:2])
        nc.const_aps.register(F32, -3.75, constv[:, 2:3])
        nc.const_aps.register(F32, 0.5, constv[:, 3:4])

        def ln_views(arena):
            RB = 24576
            return {
                "xg": lambda c: f32v(arena, c * 1024, 512),
                "x1g": lambda c: bf(arena, 16384 + c * 512, 512),
                "row_m": f32v(arena, RB, 512)[0:1, :],
                "row_v": f32v(arena, RB + 1024, 512)[0:1, :],
                "row_r": f32v(arena, RB + 2048, 512)[0:1, :],
                "row_mr": f32v(arena, RB + 3072, 512)[0:1, :],
                "xb_s": lambda i: bf(arena, RB + 4096 + i * 512, 512),
                "sq_s": lambda i: bf(arena, RB + 5120 + i * 512, 512),
                "x1st": lambda i: bf(arena, 16384 + i * 512, 512),
            }
        LNV = {"p1": ln_views(R2), "p2": ln_views(R3), "p3": ln_views(R2)}
        scst = f32v(R2, 0, 2048)
        smx = lambda i: f32v(R2, 4096 + i * 4096, 2048)
        wvh = bf(R2, 16384, 16384)
        wo_sb = lambda e: bf(R2, e * 2048, 2048)

        upb = lambda s: bf(R3, s * 8192, 8192)
        dnb = lambda s: bf(R3, 16384 + s * 8192, 8192)
        dq_qi = lambda i: i32v(R3, i * 1024, 512)
        dq_qf = lambda i: f32v(R3, 2048 + i * 1024, 512)
        dq_ob = lambda i: bf(R3, 6144 + i * 512, 512)
        dq_spl = lambda c: f32v(R3, 8192 + c * 4096, 2048)
        dq_bd = f32v(R3, 16384, 2048)[0:16, :]
        lAu = f32v(R3, 7168, 256)[0:16, :]
        lAd = f32v(R3, 20480, 1024)[0:16, :]
        lB = [f32v(R3, 22528 + i * 1024, 512)[0:16, :] for i in range(2)]
        x1v = lambda i: bf(R3, i * 8192, 8192)
        wsm = lambda e: bf(R3, e * 2048, 2048)
        cv_in = lambda i: f32v(R1, i * 4096, 2048)
        cv_ob = lambda i: bf(R1, 16384 + i * 2048, 2048)

        def wrow(kind, c):
            return 1024 * (c // 2) + 256 * kind + 128 * (c % 2)

        GROUPS8 = [list(range(8))]
        GROUPS4 = [[0, 1, 2, 3], [4, 5, 6, 7]]

        def p0(en, x):
            s_, g_, t_, a_, v_ = (en == 'S'), (en == 'G'), (en == 'T'), (en == 'A'), (en == 'V')
            if g_:
                x.dma(onec[:, :], dIN["ones_col"].ap(), "aCON")
                x.dma(oner[:, :], dIN["ones_row"].ap(), "aCON")
                x.dma(exp2[:, :], dIN["exp2"].ap(), "aCON")
                x.dma(exp2h[:, :], dIN["exp2h"].ap(), "aCON")
                x.dma(constv[:, :], dIN["constv"].ap(), "aCON")
                x.dma(bupt[:, :], dIN["b_up_t"].ap(), "aCON")
                x.dma(bdnt[:, :], dIN["b_dn_t"].ap(), "aCON")
                x.dma(lAu[:, :], dIN["AupT_sl"].ap(), "aCON")
                x.dma(lAd[:, :], dIN["AdnT_sl"].ap(), "aCON")
                x.dma(dq_bd, dIN["Bdn"].ap(), "aCON", mark="p0.consts")
            if v_:
                x.wait("aCON", "p0.consts")
                x.op("tensor_scalar_mul", bdnh[:, :], bdnt[:, :], 0.5)
                x.op("tensor_scalar_mul", dq_bd, dq_bd, 0.5,
                     sem="DVE", mark="p0.bdnh")

            for i in range(8):
                sl = i % 2
                if s_:
                    if i >= 2:
                        x.wait("DVE", f"p0.cv.{i - 2}")
                    x.dma(cv_in(sl), dIN["wqkvo_sl"].ap()[i * 128:(i + 1) * 128, :],
                          f"wCV{sl}", mark=f"p0.cvl.{i}")
                if v_:
                    x.wait(f"wCV{sl}", f"p0.cvl.{i}")
                    if i >= 2:
                        x.wait(f"aCVW{sl}", f"p0.cvw.{i - 2}")
                    if i < 2:
                        x.op("tensor_scalar_mul", cv_ob(sl), cv_in(sl), 0.125,
                             sem="DVE", mark=f"p0.cv.{i}")
                    else:
                        x.op("tensor_copy", cv_ob(sl), cv_in(sl),
                             sem="DVE", mark=f"p0.cv.{i}")
                if g_:
                    x.wait("DVE", f"p0.cv.{i}")
                    x.dma(wqkvo_c.ap()[i * 128:(i + 1) * 128, :], cv_ob(sl),
                          f"aCVW{sl}", mark=f"p0.cvw.{i}")
            if g_:
                x.wait("aCVW0", "p0.cvw.6")
                x.wait("aCVW1", "p0.cvw.7")
                x.cc("AllGather", ALU.bypass, GROUPS8,
                     wqkvo_c.ap().opt(), wqkvo_g.ap().opt(), "ccQ", mark="p0.agq")

            nblk = 0
            for hr in range(8):
                for c in range(2):
                    if s_ and hr % 2 == 0:
                        if nblk >= 2:
                            x.wait("DVE", f"p0.up.mul.{nblk - 2}")
                        q = hr // 2
                        x.dma(dq_spl(c), dIN["s_up_p"].ap()[c, q, :, :], f"wSP{c}",
                              mark=f"p0.up.sp.{c}.{hr // 2}")
                    for hb in (2 * hr, 2 * hr + 1):
                        bsl = nblk % 2
                        if s_:
                            if nblk >= 2:
                                x.wait("DVE", f"p0.up.mul.{nblk - 2}")
                            x.dma(dq_qi(bsl),
                                  dIN["wupqT"].ap()[c * 128:(c + 1) * 128,
                                                    hb * 512:(hb + 1) * 512], f"wUQ{bsl}")
                            x.dma(lB[bsl][:, :],
                                  dIN["Bup"].ap()[:, hb * 512:(hb + 1) * 512],
                                  f"wUQ{bsl}", mark=f"p0.up.qi.{nblk}")
                        if t_:
                            x.wait(f"wUQ{bsl}", f"p0.up.qi.{nblk}")
                            if nblk == 0:
                                x.wait("aCON", "p0.consts")
                            if nblk >= 2:
                                x.wait("DVE", f"p0.up.mul.{nblk - 2}")
                            x.mm(PS[7 if nblk % 2 == 0 else 5][:, :],
                                 lAu[:, c * 128:(c + 1) * 128],
                                 lB[bsl][:, :], True, True,
                                 sem="PE", mark=f"p0.up.pe.{nblk}")
                        if a_:
                            x.wait(f"wUQ{bsl}", f"p0.up.qi.{nblk}")
                            if nblk == 0:
                                x.wait("aCON", "p0.consts")
                            if nblk >= 2:
                                x.wait("DVE", f"p0.up.mul.{nblk - 2}")
                            x.op("activation", dq_qf(bsl), dq_qi(bsl), AF.Copy,
                                 bias=-7.5, sem="ACT", mark=f"p0.up.cv.{nblk}")
                        if v_:
                            x.wait("ACT", f"p0.up.cv.{nblk}")
                            x.wait("PE", f"p0.up.pe.{nblk}")
                            x.wait(f"wSP{c}", f"p0.up.sp.{c}.{hr // 2}")
                            if nblk >= 2:
                                x.wait(f"aWR{bsl}", f"p0.up.wr.{nblk - 2}")
                            off = (hb % 4) * 512
                            x.op("tensor_mul", dq_qf(bsl), dq_qf(bsl),
                                 dq_spl(c)[:, off:off + 512])
                            x.op("tensor_add", dq_ob(bsl), dq_qf(bsl),
                                 PS[7 if nblk % 2 == 0 else 5][:, :],
                                 sem="DVE", mark=f"p0.up.mul.{nblk}")
                        if g_:
                            x.wait("DVE", f"p0.up.mul.{nblk}")
                            x.dma(wup_c[hr].ap()[c * 128:(c + 1) * 128,
                                                 (hb % 2) * 512:(hb % 2) * 512 + 512],
                                  dq_ob(bsl), f"aWR{bsl}", mark=f"p0.up.wr.{nblk}")
                        nblk += 1
                if g_:
                    x.wait("aWR0", f"p0.up.wr.{nblk - 2}")
                    x.wait("aWR1", f"p0.up.wr.{nblk - 1}")
                    x.cc("AllGather", ALU.bypass, GROUPS8,
                         wup_c[hr].ap().opt(), wup_g[hr].ap().opt(),
                         f"ccU{hr}", mark=f"p0.agu.{hr}")

            nb = 0
            for j in range(8):
                if s_:
                    if j >= 2:
                        x.wait("DVE", f"p0.dn.mul.{nb - 5}")
                    else:
                        x.wait("DVE", "p0.up.mul.31")
                    x.dma(dq_spl(j % 2), dIN["s_dn_p"].ap()[j, :, :], "wSPD",
                          mark=f"p0.dn.sp.{j}")
                for db in range(4):
                    bsl = nb % 2
                    if s_:
                        if nb >= 2:
                            x.wait("DVE", f"p0.dn.mul.{nb - 2}")
                        x.dma(dq_qi(bsl),
                              dIN["wdnqT"].ap()[j * 128:(j + 1) * 128,
                                                db * 512:(db + 1) * 512],
                              f"wUQ{bsl}", mark=f"p0.dn.qi.{nb}")
                    if t_:
                        x.wait(f"wUQ{bsl}", f"p0.dn.qi.{nb}")
                        if nb == 0:
                            x.wait("DVE", "p0.bdnh")
                            x.wait("DVE", "p0.up.mul.31")
                        if nb >= 2:
                            x.wait("DVE", f"p0.dn.mul.{nb - 2}")
                        off = db * 512
                        x.mm(PS[7 if nb % 2 == 0 else 5][:, :],
                             lAd[:, j * 128:(j + 1) * 128],
                             dq_bd[:, off:off + 512], True, True,
                             sem="PE", mark=f"p0.dn.pe.{nb}")
                    if a_:
                        x.wait(f"wUQ{bsl}", f"p0.dn.qi.{nb}")
                        if nb >= 2:
                            x.wait("DVE", f"p0.dn.mul.{nb - 2}")
                        x.op("activation", dq_qf(bsl), dq_qi(bsl), AF.Copy,
                             bias=-3.75, scale=0.5,
                             sem="ACT", mark=f"p0.dn.cv.{nb}")
                    if v_:
                        x.wait("ACT", f"p0.dn.cv.{nb}")
                        x.wait("PE", f"p0.dn.pe.{nb}")
                        x.wait("wSPD", f"p0.dn.sp.{j}")
                        if nb >= 2:
                            x.wait(f"aWR{bsl}", f"p0.dn.wr.{nb - 2}")
                        else:
                            x.wait(f"aWR{bsl}", f"p0.up.wr.{30 + nb}")
                        off = db * 512
                        x.op("tensor_mul", dq_qf(bsl), dq_qf(bsl),
                             dq_spl(j % 2)[:, off:off + 512])
                        x.op("tensor_add", dq_ob(bsl), dq_qf(bsl),
                             PS[7 if nb % 2 == 0 else 5][:, :],
                             sem="DVE", mark=f"p0.dn.mul.{nb}")
                    if g_:
                        x.wait("DVE", f"p0.dn.mul.{nb}")
                        x.dma(wdn_c[db].ap()[j * 128:(j + 1) * 128, :],
                              dq_ob(bsl), f"aWR{bsl}", mark=f"p0.dn.wr.{nb}")
                    nb += 1
            if g_:
                x.wait("aWR0", f"p0.dn.wr.{nb - 2}")
                x.wait("aWR1", f"p0.dn.wr.{nb - 1}")
                for qd in range(4):
                    x.cc("AllGather", ALU.bypass, GROUPS8,
                         wdn_c[qd].ap().opt(), wdn_g[qd].ap().opt(),
                         f"ccD{qd}", mark=f"p0.agd.{qd}")

        def pred(ph, what):
            tbl = {
                ("p1", "xgfree"): [],
                ("p2", "xgfree"): [("PE", "p1.g3.dn.15")],
                ("p3", "xgfree"): [("PE", "p2.M.63"), ("DVE", "p2.out2.done")],
                ("p1", "xdata"): [],
                ("p2", "xdata"): [("aYW0", "p1.g3.yw.14"), ("aYW1", "p1.g3.yw.15")],
                ("p3", "xdata"): [("aYW0", "p2.o.yw.62"), ("aYW1", "p2.o.yw.63")],
                ("p1", "ps45"): [("DVE", "p0.dn.mul.31")],
                ("p2", "ps45"): [("DVE", "p1.ln3.var")],
                ("p3", "ps45"): [("DVE", "p2.ln3.var")],
                ("p1", "ps67"): [("DVE", "p0.dn.mul.31")],
                ("p2", "ps67"): [("DVE", "p1.ln3.nrm.15")],
                ("p3", "ps67"): [("DVE", "p2.ln3.nrm.15")],
            }
            return tbl[(ph, what)]

        def ln(en, x, ph, g, src, spill_x1, part="all"):
            g_, t_, a_, v_ = (en == 'G'), (en == 'T'), (en == 'A'), (en == 'V')
            pre = f"{ph}.ln{g}"
            V = LNV[ph]
            xg, x1g, x1st = V["xg"], V["x1g"], V["x1st"]
            row_m, row_v, row_r, row_mr = (V["row_m"], V["row_v"],
                                           V["row_r"], V["row_mr"])
            xb_s, sq_s = V["xb_s"], V["sq_s"]
            do_x = part in ("all", "x")
            do_pre = part in ("all", "pre")
            do_planes = part in ("all", "planes")
            do_norm = part in ("all", "norm")
            if g_ and do_x:
                if g > 0:
                    x.wait("DVE", f"{ph}.ln{g - 1}.nrm.15")
                else:
                    for p in pred(ph, "xgfree") + pred(ph, "xdata"):
                        x.wait(*p)
                arena = R3 if ph == "p2" else R2
                x.dma(arena[:, 0:16384].bitcast(F32).rearrange(
                          "p (c t) -> p c t", c=16),
                      src.ap().rearrange("(c p) t -> p c t", p=128)[
                          :, :, g * TG:(g + 1) * TG],
                      "aXG", mark=f"{pre}.x")
            for c in (range(NCF) if do_pre else ()):
                sl = c % 2
                if a_:
                    x.wait("aXG", f"{pre}.x")
                    if c >= 2:
                        x.wait("PE", f"{pre}.st.{c - 2}")
                    x.op("activation", sq_s(sl), xg(c), AF.Square,
                         sem="ACT", mark=f"{pre}.sq.{c}")
                if v_:
                    x.wait("aXG", f"{pre}.x")
                    if c >= 2:
                        x.wait("PE", f"{pre}.st.{c - 2}")
                    x.op("tensor_copy", xb_s(sl), xg(c),
                         sem="DVE", mark=f"{pre}.xb.{c}")
                if t_:
                    x.wait("ACT", f"{pre}.sq.{c}")
                    x.wait("DVE", f"{pre}.xb.{c}")
                    if c == 0:
                        if g > 0:
                            x.wait("DVE", f"{ph}.ln{g - 1}.var")
                        else:
                            for p in pred(ph, "ps45"):
                                x.wait(*p)
                    x.mm(PS[4][0:1, :], onec[:, :], xb_s(sl), c == 0, c == NCF - 1)
                    x.mm(PS[5][0:1, :], onec[:, :], sq_s(sl), c == 0, c == NCF - 1,
                         sem="PE", mark=f"{pre}.st.{c}")
            if v_ and do_pre:
                x.wait("PE", f"{pre}.st.15")
                x.op("tensor_scalar_mul", row_m, PS[4][0:1, :], 1.0 / F)
                x.op("tensor_scalar_mul", row_v, PS[5][0:1, :], 1.0 / F)
                x.op("tensor_mul", row_r, row_m, row_m)
                x.op("tensor_sub", row_v, row_v, row_r,
                     sem="DVE", mark=f"{pre}.var")
            if a_ and do_pre:
                x.wait("DVE", f"{pre}.var")
                x.op("activation", row_v, row_v, AF.Sqrt, bias=EPS,
                     sem="ACT", mark=f"{pre}.sqrt")
            if v_ and do_pre:
                x.wait("ACT", f"{pre}.sqrt")
                x.op("reciprocal", row_r, row_v)
                x.op("tensor_mul", row_mr, row_m, row_r,
                     sem="DVE", mark=f"{pre}.rows")
            if t_ and do_planes:
                x.wait("DVE", f"{pre}.rows")
                if g > 0:
                    x.wait("DVE", f"{ph}.ln{g - 1}.nrm.15")
                else:
                    for p in pred(ph, "ps67"):
                        x.wait(*p)
                x.mm(PS[6][:, :], oner[:, :], row_r, True, True)
                x.mm(PS[7][:, :], oner[:, :], row_mr, True, True,
                     sem="PE", mark=f"{pre}.pl")
            for c in (range(NCF) if do_norm else ()):
                if v_:
                    x.wait("PE", f"{pre}.pl")
                    if spill_x1 and c >= 2:
                        x.wait(f"aX1W{c % 2}", f"{pre}.x1w.{c - 2}")
                    elif spill_x1 and g > 0:
                        x.wait(f"aX1W{c % 2}", f"{ph}.ln{g - 1}.x1w.{14 + c}")
                    if (not spill_x1) and g == 0 and c == 0 and ph == "p3":
                        x.wait("aX1W0", "p2.ln3.x1w.14")
                        x.wait("aX1W1", "p2.ln3.x1w.15")
                    x.op("tensor_mul", xg(c), xg(c), PS[6][:, :])
                    dst = x1st(c % 2) if spill_x1 else x1g(c)
                    x.op("tensor_sub", dst, xg(c), PS[7][:, :],
                         sem="DVE", mark=f"{pre}.nrm.{c}")
                if spill_x1 and g_:
                    x.wait("DVE", f"{pre}.nrm.{c}")
                    x.dma(x1_d.ap()[c * 128:(c + 1) * 128, g * TG:(g + 1) * TG],
                          x1st(c % 2), f"aX1W{c % 2}", mark=f"{pre}.x1w.{c}")

        def ffn(en, x, ph, src, dst):
            s_, g_, t_, a_, v_ = (en == 'S'), (en == 'G'), (en == 'T'), (en == 'A'), (en == 'V')
            is_p1 = (ph == "p1")
            x1g = LNV[ph]["x1g"]
            ln(en, x, ph, 0, src, spill_x1=False)
            for g in range(NG):
                for hr in range(16):
                    sl = hr % 2
                    if s_:
                        if hr >= 2:
                            x.wait("PE", f"{ph}.g{g}.up.r{hr - 2}.done")
                        elif g > 0:
                            x.wait("PE", f"{ph}.g{g - 1}.up.r{14 + hr}.done")
                        elif is_p1:
                            x.wait("aWR0", "p0.dn.wr.30")
                            x.wait("aWR1", "p0.dn.wr.31")
                        else:
                            x.wait("PE", "p2.M.63")
                        x.wait(f"ccU{hr // 2}", f"p0.agu.{hr // 2}")
                        for c in range(NCF):
                            mk = f"{ph}.g{g}.upl.{hr}" if c == NCF - 1 else None
                            x.dma(upb(sl)[:, c * 512:(c + 1) * 512],
                                  wup_g[hr // 2].ap()[
                                      c * 128:(c + 1) * 128,
                                      (hr % 2) * 512:(hr % 2) * 512 + 512],
                                  f"wUP{sl}", mark=mk)
                    if t_:
                        x.wait(f"wUP{hr % 2}", f"{ph}.g{g}.upl.{hr}")
                        if hr == 0:
                            x.wait("DVE", f"{ph}.ln{g}.nrm.15")
                        for ht in range(4):
                            htg = hr * 4 + ht
                            bank = htg % 4
                            if htg >= 4:
                                x.wait("ACT", f"{ph}.g{g}.upe.{htg - 4}")
                            elif g > 0:
                                x.wait("DVE", f"{ph}.g{g - 1}.dne.{12 + htg}")
                            elif ph == "p3":
                                x.wait("DVE", "p2.out2.done")
                            for c in range(NCF):
                                last = (c == NCF - 1)
                                mk = None
                                if last:
                                    mk = (f"{ph}.g{g}.up.r{hr}.done" if ht == 3
                                          else f"{ph}.g{g}.up.{htg}")
                                x.mm(PS[bank][:, :],
                                     upb(sl)[:, c * 512 + ht * 128:
                                             c * 512 + ht * 128 + 128],
                                     x1g(c), c == 0, last,
                                     sem="PE" if last else None, mark=mk)
                    if a_:
                        for ht in range(4):
                            htg = hr * 4 + ht
                            bank = htg % 4
                            mk = (f"{ph}.g{g}.up.r{hr}.done" if ht == 3
                                  else f"{ph}.g{g}.up.{htg}")
                            x.wait("PE", mk)
                            if g == 0 and htg == 0:
                                if is_p1:
                                    x.wait("aCVW0", "p0.cvw.6")
                                    x.wait("aCVW1", "p0.cvw.7")
                                else:
                                    x.wait("PE", "p2.o.15.done")
                            x.op("activation", bf(R1, htg * 512, 512),
                                 PS[bank][:, :], AF.Relu,
                                 bias=bupt[:, htg:htg + 1],
                                 sem="ACT", mark=f"{ph}.g{g}.upe.{htg}")
                if g < 3:
                    ln(en, x, ph, g + 1, src, spill_x1=False, part="x")
                    ln(en, x, ph, g + 1, src, spill_x1=False, part="pre")
                    ln(en, x, ph, g + 1, src, spill_x1=False, part="planes")
                    ln(en, x, ph, g + 1, src, spill_x1=False, part="norm")
                for ss in range(16):
                    sl = ss % 2
                    qd, qc = ss // 4, ss % 4
                    if s_:
                        if ss >= 2:
                            x.wait("PE", f"{ph}.g{g}.dn.{ss - 2}")
                        elif g > 0:
                            x.wait("PE", f"{ph}.g{g - 1}.dn.{14 + ss}")
                        elif is_p1:
                            x.wait("aWR0", "p0.dn.wr.30")
                            x.wait("aWR1", "p0.dn.wr.31")
                        else:
                            x.wait("PE", "p2.M.63")
                        x.wait(f"ccD{qd}", f"p0.agd.{qd}")
                        x.dma(dnb(sl).rearrange("p (h c) -> p h c", h=64),
                              wdn_g[qd].ap().rearrange("(h p) c -> p h c", p=128)[
                                  :, :, qc * 128:(qc + 1) * 128],
                              f"wDN{sl}", mark=f"{ph}.g{g}.dnl.{ss}")
                    if g_:
                        if ss >= 2:
                            x.wait(f"aYW{sl}", f"{ph}.g{g}.yw.{ss - 2}")
                        elif g > 0:
                            x.wait(f"aYW{sl}", f"{ph}.g{g - 1}.yw.{14 + ss}")
                        elif ph == "p3":
                            x.wait(f"aYW{sl}", f"p2.o.yw.{62 + ss}")
                        x.dma(xres[sl][:, :],
                              src.ap()[ss * 128:(ss + 1) * 128,
                                       g * TG:(g + 1) * TG],
                              f"aXR{sl}", mark=f"{ph}.g{g}.xr.{ss}")
                    if t_:
                        bank = ss % 4
                        x.wait(f"wDN{ss % 2}", f"{ph}.g{g}.dnl.{ss}")
                        if ss == 0:
                            x.wait("ACT", f"{ph}.g{g}.upe.63")
                        if ss >= 4:
                            x.wait("DVE", f"{ph}.g{g}.dne.{ss - 4}")
                        elif g > 0:
                            x.wait("DVE", f"{ph}.g{g - 1}.dne.{12 + ss}")
                        for h in range(64):
                            last = (h == 63)
                            x.mm(PS[bank][:, :],
                                 dnb(sl)[:, h * 128:(h + 1) * 128],
                                 bf(R1, h * 512, 512), h == 0, last,
                                 sem="PE" if last else None,
                                 mark=f"{ph}.g{g}.dn.{ss}" if last else None)
                    if v_:
                        bank = ss % 4
                        x.wait("PE", f"{ph}.g{g}.dn.{ss}")
                        x.wait(f"aXR{ss % 2}", f"{ph}.g{g}.xr.{ss}")
                        if ss >= 2:
                            x.wait(f"aYW{ss % 2}", f"{ph}.g{g}.yw.{ss - 2}")
                        elif g > 0:
                            x.wait(f"aYW{ss % 2}", f"{ph}.g{g - 1}.yw.{14 + ss}")
                        x.op("scalar_tensor_tensor", xres[sl][:, :],
                             PS[bank][:, :], bdnh[:, ss:ss + 1], xres[sl][:, :],
                             ALU.add, ALU.add,
                             sem="DVE", mark=f"{ph}.g{g}.dne.{ss}")
                    if g_:
                        x.wait("DVE", f"{ph}.g{g}.dne.{ss}")
                        x.dma(dst.ap()[ss * 128:(ss + 1) * 128,
                                       g * TG:(g + 1) * TG],
                              xres[sl][:, :], f"aYW{sl}", mark=f"{ph}.g{g}.yw.{ss}")
        def attn(en, x):
            s_, g_, t_, a_, v_ = (en == 'S'), (en == 'G'), (en == 'T'), (en == 'A'), (en == 'V')
            if s_:
                x.wait("PE", "p1.g3.up.r15.done")
                x.wait("DVE", "p1.ln3.nrm.15")
                for c in range(NCF):
                    x.dma(bf(R2, c * 2048, 2048),
                          wqkvo_g.ap()[wrow(0, c):wrow(0, c) + 128, :],
                          "wW", mark="p2.q.wl" if c == NCF - 1 else None)
            def q_or_k(wn, arena, st_range):
                for st_ in st_range:
                    sl = st_ % 2
                    if g_:
                        if st_ >= 2:
                            x.wait("PE", f"p2.{wn}.{st_ - 2}.done")
                        elif wn == "k":
                            x.wait("PE", f"p2.q.{14 + st_}.done")
                        x.wait("aX1W0", f"p2.ln{st_ // 4}.x1w.14")
                        x.wait("aX1W1", f"p2.ln{st_ // 4}.x1w.15")
                        x.dma(x1tb[sl][:, :].rearrange("p (c s) -> p c s", c=16),
                              x1_d.ap().rearrange("(c p) s -> p c s", p=128)[
                                  :, :, st_ * 128:(st_ + 1) * 128],
                              f"aX1R{sl}", mark=f"p2.{wn}.x1.{st_}")
                    if t_:
                        if st_ == 0:
                            x.wait("wW", f"p2.{wn}.wl")
                            if wn == "q":
                                x.wait("ACT", "p1.g3.upe.63")
                                x.wait("DVE", "p1.g3.dne.15")
                        x.wait(f"aX1R{sl}", f"p2.{wn}.x1.{st_}")
                        x.wait("aX1W0", f"p2.ln{st_ // 4}.x1w.14")
                        x.wait("aX1W1", f"p2.ln{st_ // 4}.x1w.15")
                        for eg in range(4):
                            n = st_ * 4 + eg
                            bank = n % 4
                            if n >= 4:
                                x.wait("ACT", f"p2.{wn}.ev.{n - 4}")
                            elif wn == "k":
                                x.wait("ACT", f"p2.q.ev.{60 + n}")
                            for c in range(NCF):
                                last = (c == NCF - 1)
                                mk = None
                                if last:
                                    mk = (f"p2.{wn}.{st_}.done" if eg == 3
                                          else f"p2.{wn}.{st_}.{eg}")
                                x.mm(PS[bank][:, :],
                                     x1tb[sl][:, c * 128:(c + 1) * 128],
                                     bf(R2, c * 2048 + eg * 512, 512),
                                     c == 0, last,
                                     sem="PE" if last else None, mark=mk)
                    if a_:
                        for eg in range(4):
                            n = st_ * 4 + eg
                            bank = n % 4
                            mk = (f"p2.{wn}.{st_}.done" if eg == 3
                                  else f"p2.{wn}.{st_}.{eg}")
                            x.wait("PE", mk)
                            if wn == "k" and n == 0:
                                x.wait("PE", "p1.g3.dn.15")
                                x.wait("aX1W0", "p2.ln3.x1w.14")
                                x.wait("aX1W1", "p2.ln3.x1w.15")
                            x.op("activation",
                                 arena[:, st_ * 2048 + eg * 512:
                                       st_ * 2048 + eg * 512 + 512],
                                 PS[bank][:, :], AF.Copy,
                                 sem="ACT", mark=f"p2.{wn}.ev.{n}")

            for g in range(NG):
                ln(en, x, "p2", g, x_a, spill_x1=True)
                q_or_k("q", R1, range(4 * g, 4 * g + 4))
            if s_:
                x.wait("PE", "p2.q.15.done")
                for c in range(NCF):
                    x.dma(bf(R2, c * 2048, 2048),
                          wqkvo_g.ap()[wrow(1, c):wrow(1, c) + 128, :],
                          "wW", mark="p2.k.wl" if c == NCF - 1 else None)
            q_or_k("k", R3, range(16))
            for et in range(16):
                for fg in range(4):
                    n = et * 4 + fg
                    bank = n % 4
                    if t_:
                        if n == 0:
                            x.wait("ACT", "p2.k.ev.63")
                        if n >= 4:
                            x.wait("ACT", f"p2.sc.ev.{n - 4}")
                        for st_ in range(16):
                            last = (st_ == 15)
                            x.mm(PS[bank][:, :],
                                 R1[:, st_ * 2048 + et * 128:
                                    st_ * 2048 + et * 128 + 128],
                                 R3[:, st_ * 2048 + fg * 512:
                                    st_ * 2048 + fg * 512 + 512],
                                 st_ == 0, last,
                                 sem="PE" if last else None,
                                 mark=f"p2.sc.{n}" if last else None)
                    if a_:
                        x.wait("PE", f"p2.sc.{n}")
                        if et >= 1 and fg == 0:
                            x.wait("aSC", f"p2.scw.{et - 1}")
                        x.op("activation", scst[:, fg * 512:(fg + 1) * 512],
                             PS[bank][:, :], AF.Copy,
                             sem="ACT", mark=f"p2.sc.ev.{n}")
                if g_:
                    x.wait("ACT", f"p2.sc.ev.{et * 4 + 3}")
                    x.dma(sc_p.ap()[et * 128:(et + 1) * 128, :], scst,
                          "aSC", mark=f"p2.scw.{et}")
                    if et % 4 == 3:
                        q = et // 4
                        x.wait("aSC", f"p2.scw.{et}")
                        x.cc("AllReduce", ALU.add, GROUPS4,
                             sc_p.ap()[q * 512:(q + 1) * 512, :].opt(),
                             sc_r.ap()[q * 512:(q + 1) * 512, :].opt(),
                             f"ccS{q}", mark=f"p2.ar.{q}")
            for half in range(2):
                if s_:
                    if half == 0:
                        x.wait("PE", "p2.k.15.done")
                    else:
                        x.wait("PE", "p2.v.c03")
                    for c in range(NCF):
                        x.dma(wvh[:, c * 1024:(c + 1) * 1024],
                              wqkvo_g.ap()[wrow(2, c):wrow(2, c) + 128,
                                           half * 1024:half * 1024 + 1024],
                              "wW", mark=f"p2.v.wl.{half}" if c == NCF - 1 else None)
                for sg in range(NG):
                    vsl = sg % 2
                    if g_:
                        if half == 0 and sg == 0:
                            x.wait("PE", "p2.sc.63")
                        nprev = half * 4 + sg - 2
                        if nprev >= 0:
                            hp, sp_ = nprev // 4, nprev % 4
                            x.wait("PE", f"p2.v.c{hp}{sp_}")
                        x.dma(x1v(vsl).rearrange("p (c s) -> p c s", c=16),
                              x1_d.ap().rearrange("(c p) s -> p c s", p=128)[
                                  :, :, sg * TG:(sg + 1) * TG],
                              f"aXV{vsl}", mark=f"p2.v.x1.{half}.{sg}")
                    if t_:
                        x.wait("wW", f"p2.v.wl.{half}")
                        x.wait(f"aXV{vsl}", f"p2.v.x1.{half}.{sg}")
                        for fi in range(8):
                            ft = half * 8 + fi
                            n = half * 32 + sg * 8 + fi
                            bank = n % 4
                            if n >= 4:
                                x.wait("ACT", f"p2.v.ev.{n - 4}")
                            elif n < 4:
                                x.wait("ACT", f"p2.sc.ev.{60 + n}")
                            for c in range(NCF):
                                last = (c == NCF - 1)
                                mk = None
                                if last:
                                    mk = (f"p2.v.c{half}{sg}" if fi == 7
                                          else f"p2.v.n.{n}")
                                x.mm(PS[bank][:, :],
                                     wvh[:, c * 1024 + fi * 128:
                                         c * 1024 + fi * 128 + 128],
                                     x1v(vsl)[:, c * 512:(c + 1) * 512],
                                     c == 0, last,
                                     sem="PE" if last else None, mark=mk)
                    if a_:
                        for fi in range(8):
                            ft = half * 8 + fi
                            n = half * 32 + sg * 8 + fi
                            bank = n % 4
                            mk = (f"p2.v.c{half}{sg}" if fi == 7
                                  else f"p2.v.n.{n}")
                            x.wait("PE", mk)
                            if n == 0:
                                x.wait("PE", "p2.sc.63")
                                x.wait("aX1W0", "p2.ln3.x1w.14")
                                x.wait("aX1W1", "p2.ln3.x1w.15")
                            x.op("activation",
                                 R1[:, ft * 2048 + sg * 512:
                                    ft * 2048 + sg * 512 + 512],
                                 PS[bank][:, :], AF.Copy,
                                 sem="ACT", mark=f"p2.v.ev.{n}")
            SM_ORDER = list(range(8, 16)) + list(range(8))
            for i, et in enumerate(SM_ORDER):
                q = et // 4
                sl = i % 2
                if g_:
                    x.wait(f"ccS{q}", f"p2.ar.{q}")
                    if i >= 2:
                        x.wait("ACT", f"p2.sm.exp.{SM_ORDER[i - 2]}")
                    if i == 0:
                        x.wait("PE", "p2.sc.63")
                    x.dma(smx(sl), sc_r.ap()[et * 128:(et + 1) * 128, :],
                          f"aSM{sl}", mark=f"p2.sm.l.{et}")
                if v_:
                    x.wait(f"aSM{sl}", f"p2.sm.l.{et}")
                    if i > 0:
                        x.wait("ACT", f"p2.sm.exp.{SM_ORDER[i - 1]}")
                    x.op("reduce_max", nmax[:, :], smx(sl), axis=AX.X,
                         negate=True, sem="DVE", mark=f"p2.sm.mx.{et}")
                if a_:
                    x.wait("DVE", f"p2.sm.mx.{et}")
                    if i == 0:
                        x.wait("PE", "p2.sc.63")
                    if et < 8:
                        x.wait("PE", "p2.v.c13")
                    x.op("activation", wsm(et), smx(sl), AF.Exp,
                         bias=nmax[:, :], accum_out=rs_c[:, :],
                         sem="ACT", mark=f"p2.sm.exp.{et}")
                if v_:
                    x.wait("ACT", f"p2.sm.exp.{et}")
                    x.op("reciprocal", rr_all[:, et:et + 1], rs_c[:, :],
                         sem="DVE", mark=f"p2.sm.rr.{et}")
            for e in list(range(8, 16)) + list(range(8)):
                if s_:
                    if e == 8:
                        x.wait("PE", "p2.v.c13")
                    if e == 0:
                        x.wait("ACT", "p2.sm.exp.7")
                        x.wait("aSC", "p2.scw.15")
                    x.dma(wo_sb(e),
                          wqkvo_g.ap()[wrow(3, e):wrow(3, e) + 128, :],
                          "wW", mark=f"p2.wo.l.{e}")
                if v_:
                    x.wait("wW", "p2.wo.l.7")
                    x.wait("DVE", "p2.sm.rr.7")
                    x.op("tensor_scalar_mul", wo_sb(e), wo_sb(e),
                         rr_all[:, e:e + 1],
                         sem="DVE", mark=f"p2.wo.f.{e}")
            for eo in range(4):
                for ft in range(16):
                    n = eo * 16 + ft
                    bank = n % 4
                    if t_:
                        if n == 0:
                            x.wait("ACT", "p2.sm.exp.7")
                            x.wait("DVE", "p2.wo.f.7")
                        if n >= 4:
                            x.wait("ACT", f"p2.M.ev.{n - 4}")
                        for e in range(16):
                            last = (e == 15)
                            x.mm(PS[bank][:, :],
                                 R3[:, e * 2048 + ft * 128:
                                    e * 2048 + ft * 128 + 128],
                                 wo_sb(e)[:, eo * 512:(eo + 1) * 512],
                                 e == 0, last,
                                 sem="PE" if last else None,
                                 mark=f"p2.M.{n}" if last else None)
                    if a_:
                        x.wait("PE", f"p2.M.{n}")
                        if n >= 2:
                            x.wait(f"aMW{n % 2}", f"p2.M.w.{n - 2}")
                        x.op("activation", mstg[n % 2][:, :], PS[bank][:, :],
                             AF.Copy, sem="ACT", mark=f"p2.M.ev.{n}")
                    if g_:
                        x.wait("ACT", f"p2.M.ev.{n}")
                        x.dma(M_d.ap()[ft * 128:(ft + 1) * 128,
                                       eo * 512:(eo + 1) * 512],
                              mstg[n % 2][:, :], f"aMW{n % 2}", mark=f"p2.M.w.{n}")
            for et in range(16):
                sl = et % 2
                if s_:
                    if et >= 2:
                        x.wait("PE", f"p2.o.{et - 2}.done")
                    else:
                        x.wait("PE", "p2.k.15.done")
                    x.wait("aMW0", f"p2.M.w.{(et // 4) * 16 + 14}")
                    x.wait("aMW1", f"p2.M.w.{(et // 4) * 16 + 15}")
                    x.dma(x1tb[sl][:, :].rearrange("p (c e) -> p c e", c=16),
                          M_d.ap().rearrange("(c p) e -> p c e", p=128)[
                              :, :, et * 128:(et + 1) * 128],
                          f"wM{sl}", mark=f"p2.o.ml.{et}")
                for sg in range(4):
                    n = et * 4 + sg
                    bank = n % 4
                    amark = "p2.out2.done" if n == 63 else f"p2.o.add.{n}"
                    if g_:
                        if n >= 2:
                            x.wait(f"aYW{n % 2}", f"p2.o.yw.{n - 2}")
                        else:
                            x.wait("aMW0", "p2.M.w.62")
                            x.wait("aMW1", "p2.M.w.63")
                            x.wait(f"aYW{n % 2}", f"p1.g3.yw.{14 + n}")
                        x.dma(xres[n % 2][:, :],
                              x_a.ap()[et * 128:(et + 1) * 128,
                                       sg * TG:(sg + 1) * TG],
                              f"aXR{n % 2}", mark=f"p2.o.xr.{n}")
                    if t_:
                        if n == 0:
                            x.wait("ACT", "p2.M.ev.63")
                        x.wait(f"wM{sl}", f"p2.o.ml.{et}")
                        if n >= 4:
                            x.wait("DVE", f"p2.o.add.{n - 4}")
                        for c in range(NCF):
                            last = (c == NCF - 1)
                            mk = None
                            if last:
                                mk = (f"p2.o.{et}.done" if sg == 3
                                      else f"p2.o.n.{n}")
                            x.mm(PS[bank][:, :],
                                 x1tb[sl][:, c * 128:(c + 1) * 128],
                                 R1[:, c * 2048 + sg * 512:
                                    c * 2048 + sg * 512 + 512],
                                 c == 0, last,
                                 sem="PE" if last else None, mark=mk)
                    if v_:
                        mk = f"p2.o.{et}.done" if sg == 3 else f"p2.o.n.{n}"
                        x.wait("PE", mk)
                        x.wait(f"aXR{n % 2}", f"p2.o.xr.{n}")
                        if n >= 2:
                            x.wait(f"aYW{n % 2}", f"p2.o.yw.{n - 2}")
                        x.op("tensor_add", xres[n % 2][:, :], xres[n % 2][:, :],
                             PS[bank][:, :], sem="DVE", mark=amark)
                    if g_:
                        x.wait("DVE", amark)
                        x.dma(x_b.ap()[et * 128:(et + 1) * 128,
                                       sg * TG:(sg + 1) * TG],
                              xres[n % 2][:, :], f"aYW{n % 2}", mark=f"p2.o.yw.{n}")

        def emit_all(en, eng):
            x = E(ctx, eng, en)
            p0(en, x)
            if n_phases >= 1:
                ffn(en, x, "p1", dIN["x_in"], x_a if n_phases > 1 else y_out)
                if ctx.dry and en == 'T':
                    ctx.plan["p1.x1.dead"] = ctx.plan["p1.g3.up.r15.done"]
            if n_phases >= 2:
                attn(en, x)
                if n_phases == 2 and en == 'G':
                    x.wait("aYW0", "p2.o.yw.62")
                    x.wait("aYW1", "p2.o.yw.63")
                    x.dma(y_out.ap(), x_b.ap(), "aYW0", mark="fin.copy")
            if n_phases >= 3:
                ffn(en, x, "p3", x_b, y_out)
            if en == 'G':
                for s in W_SEMS + A_SEMS:
                    x.wait_total(s)

        ctx.dry = True
        for en in ("S", "G", "T", "A", "V"):
            emit_all(en, None)
        ctx.totals = dict(ctx.sems)

        ctx.dry = False
        with nc.Block() as block:
            @block.sync
            def _(sync):
                ctx.sems = {}
                emit_all("S", sync)

            @block.gpsimd
            def _(gpsimd):
                ctx.sems = {}
                emit_all("G", gpsimd)

            @block.tensor
            def _(tensor):
                ctx.sems = {}
                emit_all("T", tensor)

            @block.scalar
            def _(scalar):
                ctx.sems = {}
                emit_all("A", scalar)

            @block.vector
            def _(vector):
                ctx.sems = {}
                emit_all("V", vector)

    return nc


def make_inputs(L0_attn_x, L0_attn_Wq, L0_attn_Wk, L0_attn_Wv, L0_attn_Wo,
                L0_ffn_w_up_q, L0_ffn_w_up_scale, L0_ffn_b_up,
                L0_ffn_w_up_lora_a, L0_ffn_w_up_lora_b,
                L0_ffn_w_down_q, L0_ffn_w_down_scale, L0_ffn_b_down,
                L0_ffn_w_down_lora_a, L0_ffn_w_down_lora_b):
    C = np.ascontiguousarray
    x = np.asarray(L0_attn_x)
    wupqT = np.asarray(L0_ffn_w_up_q).T
    s_upT = np.asarray(L0_ffn_w_up_scale).T
    wdnqT = np.asarray(L0_ffn_w_down_q).T
    s_dnT = np.asarray(L0_ffn_w_down_scale).T
    Wq = np.asarray(L0_attn_Wq); Wk = np.asarray(L0_attn_Wk)
    Wv = np.asarray(L0_attn_Wv); Wo = np.asarray(L0_attn_Wo)
    b_up = np.asarray(L0_ffn_b_up); b_dn = np.asarray(L0_ffn_b_down)
    AupT = np.asarray(L0_ffn_w_up_lora_a).T
    Bup = np.asarray(L0_ffn_w_up_lora_b)
    AdnT = np.asarray(L0_ffn_w_down_lora_a).T
    Bdn = np.asarray(L0_ffn_w_down_lora_b)
    b_up_t = C(b_up.reshape(64, 128).T).astype(np.float32)
    b_dn_t = C(b_dn.reshape(16, 128).T).astype(np.float32)
    ones_col = np.ones((128, 1), BF)
    ones_row = np.ones((1, 128), np.float32)
    exp2 = np.zeros((2, 128), np.float32)
    exp2[0, :64] = 1.0
    exp2[1, 64:] = 1.0
    exp2h = exp2 * 0.5
    constv_np = np.zeros((128, 4), np.float32)
    constv_np[:, 0] = -7.5
    constv_np[:, 1] = EPS
    constv_np[:, 2] = -3.75
    constv_np[:, 3] = 0.5

    in_maps = []
    for c in range(8):
        b, sl = c // 4, c % 4
        x_fm = C(x[b, sl * T:(sl + 1) * T, :].T).astype(np.float32)
        sp = np.zeros((2, 4, 128, 2048), np.float32)
        for ct in range(2):
            g0 = 4 * c + 2 * ct
            for q in range(4):
                sp[ct, q, 0:64, :] = s_upT[g0, q * 2048:(q + 1) * 2048]
                sp[ct, q, 64:128, :] = s_upT[g0 + 1, q * 2048:(q + 1) * 2048]
        sdp = np.zeros((8, 128, F), np.float32)
        for j in range(8):
            g0 = (1024 * c + 128 * j) // 64
            sdp[j, 0:64, :] = s_dnT[g0, :]
            sdp[j, 64:128, :] = s_dnT[g0 + 1, :]
        wqkvo_sl = np.concatenate([
            C(Wq[256 * c:256 * (c + 1), :]),
            C(Wk[256 * c:256 * (c + 1), :]),
            C(Wv[256 * c:256 * (c + 1), :]),
            C(Wo[256 * c:256 * (c + 1), :])], axis=0).astype(np.float32)
        in_maps.append({
            "x_in": x_fm,
            "wupqT": C(wupqT[256 * c:256 * (c + 1), :]).astype(np.int32),
            "wdnqT": C(wdnqT[1024 * c:1024 * (c + 1), :]).astype(np.int32),
            "s_up_p": sp, "s_dn_p": sdp,
            "wqkvo_sl": wqkvo_sl,
            "b_up_t": b_up_t, "b_dn_t": b_dn_t,
            "AupT_sl": C(AupT[:, 256 * c:256 * (c + 1)]).astype(np.float32),
            "Bup": C(Bup).astype(np.float32),
            "AdnT_sl": C(AdnT[:, 1024 * c:1024 * (c + 1)]).astype(np.float32),
            "Bdn": C(Bdn).astype(np.float32),
            "ones_col": ones_col, "ones_row": ones_row,
            "exp2": exp2, "exp2h": exp2h,
            "constv": constv_np,
        })
    return in_maps


N_PHASES = 3


def kernel(**inputs):
    in_maps = make_inputs(**inputs)
    nc = build_nc(N_PHASES)
    res = run_bass_kernel_spmd(nc, in_maps, core_ids=list(range(8)))
    out = np.zeros((2, 8192, 2048), np.float32)
    for c in range(8):
        b, sl = c // 4, c % 4
        out[b, sl * T:(sl + 1) * T, :] = res.results[c]["y_out"].T
    return out


# version 03
# speedup vs baseline: 1.0630x; 1.0174x over previous
import contextlib
import numpy as np
import ml_dtypes
import concourse.bass as bass
import concourse.mybir as mybir
from concourse.bass_utils import run_bass_kernel_spmd

F32 = mybir.dt.float32
BF16 = mybir.dt.bfloat16
I32 = mybir.dt.int32
AF = mybir.ActivationFunctionType
ALU = mybir.AluOpType
AX = mybir.AxisListType
BF = ml_dtypes.bfloat16

F = 2048
T = 2048
H = 8192
NCF = 16
NG = 4
TG = 512
EPS = 1e-5


class Ctx:
    def __init__(self):
        self.sems = {}
        self.plan = {}
        self.dry = True
        self.S = {}
        self.totals = {}

    def inc(self, sem, n, mark=None):
        self.sems[sem] = self.sems.get(sem, 0) + n
        if mark is not None:
            if self.dry:
                assert mark not in self.plan, f"dup mark {mark}"
                self.plan[mark] = self.sems[sem]
            else:
                assert self.plan[mark] == self.sems[sem], (
                    f"plan mismatch {mark}: plan={self.plan[mark]} "
                    f"emit={self.sems[sem]}")
        return self.sems[sem]


class E:

    AUTO = {"V": "DVE", "A": "ACT"}

    def __init__(self, ctx, eng, en=None):
        self.ctx = ctx
        self.eng = eng
        self.auto = self.AUTO.get(en)

    def wait(self, sem, mark):
        if self.ctx.dry:
            assert isinstance(mark, str)
            return
        v = self.ctx.plan[mark]
        if v > 0:
            self.eng.wait_ge(self.ctx.S[sem], v)

    def wait_total(self, sem):
        if self.ctx.dry:
            return
        v = self.ctx.totals.get(sem, 0)
        if v > 0:
            self.eng.wait_ge(self.ctx.S[sem], v)

    def dma(self, out, in_, sem, mark=None, **kw):
        if self.ctx.dry:
            self.ctx.inc(sem, 16, mark)
            return
        ins = self.eng.dma_start(out=out, in_=in_, **kw)
        ins.then_inc(self.ctx.S[sem], 16)
        self.ctx.inc(sem, 16, mark)

    def cc(self, kind, op, groups, in_ap, out_ap, sem, mark=None):
        if self.ctx.dry:
            self.ctx.inc(sem, 1, mark)
            return
        ins = self.eng.collective_compute(
            kind, op, replica_groups=groups, ins=[in_ap], outs=[out_ap])
        ins.then_inc(self.ctx.S[sem])
        self.ctx.inc(sem, 1, mark)

    def op(self, fn, *args, sem=None, mark=None, **kw):
        if self.auto is not None:
            prev = self.ctx.sems.get(self.auto, 0)
            if not self.ctx.dry and prev > 0:
                self.eng.wait_ge(self.ctx.S[self.auto], prev)
            if sem is None:
                sem = self.auto
            else:
                assert sem == self.auto, (sem, self.auto)
        if self.ctx.dry:
            if sem is not None:
                self.ctx.inc(sem, 1, mark)
            return
        ins = getattr(self.eng, fn)(*args, **kw)
        if sem is not None:
            ins.then_inc(self.ctx.S[sem])
            self.ctx.inc(sem, 1, mark)

    def mm(self, out, lhsT, rhs, start, stop, sem=None, mark=None):
        self.op("matmul", out, lhsT, rhs, start=start, stop=stop,
                sem=sem, mark=mark)


def build_nc(n_phases=3):
    nc = bass.Bass("TRN2", target_bir_lowering=False, debug=False)

    dIN = {}
    def din(name, shape, dt):
        dIN[name] = nc.dram_tensor(name, shape, dt, kind="ExternalInput")
        return dIN[name]

    din("x_in", [F, T], F32)
    din("wupqT", [256, H], I32)
    din("wdnqT", [1024, F], I32)
    din("s_up_p", [2, 4, 128, 2048], F32)
    din("s_dn_p", [8, 128, F], F32)
    din("wqkvo_sl", [1024, F], F32)
    din("b_up_t", [128, 64], F32)
    din("b_dn_t", [128, 16], F32)
    din("AupT_sl", [16, 256], F32)
    din("Bup", [16, H], F32)
    din("AdnT_sl", [16, 1024], F32)
    din("Bdn", [16, F], F32)
    din("ones_col", [128, 1], BF16)
    din("ones_row", [1, 128], F32)
    din("exp2", [2, 128], F32)
    din("exp2h", [2, 128], F32)
    din("constv", [128, 4], F32)

    y_out = nc.dram_tensor("y_out", [F, T], F32, kind="ExternalOutput")

    wup_c = [nc.dram_tensor(f"wup_c{r}", [256, 1024], BF16) for r in range(8)]
    wup_g = [nc.dram_tensor(f"wup_g{r}", [F, 1024], BF16) for r in range(8)]
    wdn_c = [nc.dram_tensor(f"wdn_c{r}", [1024, 512], BF16) for r in range(4)]
    wdn_g = [nc.dram_tensor(f"wdn_g{r}", [H, 512], BF16) for r in range(4)]
    wqkvo_c = nc.dram_tensor("wqkvo_c", [1024, F], BF16)
    wqkvo_g = nc.dram_tensor("wqkvo_g", [H, F], BF16)
    x_a = nc.dram_tensor("x_a", [F, T], F32)
    x_b = nc.dram_tensor("x_b", [F, T], F32)
    x1_d = nc.dram_tensor("x1_d", [F, T], BF16)
    sc_p = nc.dram_tensor("sc_p", [F, F], F32)
    sc_r = nc.dram_tensor("sc_r", [F, F], F32)
    M_d = nc.dram_tensor("M_d", [F, F], BF16)

    ctx = Ctx()
    W_SEMS = (["wCV0", "wCV1", "wSP0", "wSP1", "wSPD", "wUQ0", "wUQ1",
               "wUP0", "wUP1", "wDN0", "wDN1", "wW", "wM0", "wM1", "wSC"])
    A_SEMS = (["aCON", "aCVW0", "aCVW1", "aWR0", "aWR1", "aXG", "aXR0", "aXR1",
               "aX1W0", "aX1W1", "aX1R0", "aX1R1", "aXV0", "aXV1", "aSC",
               "aSM0", "aSM1", "aMW0", "aMW1", "aYW0", "aYW1"])
    sem_names = (W_SEMS + A_SEMS + ["PE", "ACT", "DVE"]
                 + [f"ccU{r}" for r in range(8)]
                 + [f"ccD{r}" for r in range(4)]
                 + ["ccQ"] + [f"ccS{r}" for r in range(4)])

    with contextlib.ExitStack() as st:
        R1 = st.enter_context(nc.sbuf_tensor("R1", [128, 32768], BF16))
        R2 = st.enter_context(nc.sbuf_tensor("R2", [128, 32768], BF16))
        R3 = st.enter_context(nc.sbuf_tensor("R3", [128, 32768], BF16))
        onec = st.enter_context(nc.sbuf_tensor("onec", [128, 1], BF16))
        oner = st.enter_context(nc.sbuf_tensor("oner", [1, 128], F32))
        exp2 = st.enter_context(nc.sbuf_tensor("exp2_sb", [2, 128], F32))
        exp2h = st.enter_context(nc.sbuf_tensor("exp2h_sb", [2, 128], F32))
        bupt = st.enter_context(nc.sbuf_tensor("bupt", [128, 64], F32))
        bdnt = st.enter_context(nc.sbuf_tensor("bdnt", [128, 16], F32))
        bdnh = st.enter_context(nc.sbuf_tensor("bdnh", [128, 16], F32))
        xres = [st.enter_context(nc.sbuf_tensor(f"xres{i}", [128, 512], F32))
                for i in range(2)]
        x1tb = [st.enter_context(nc.sbuf_tensor(f"x1tb{i}", [128, 2048], BF16))
                for i in range(2)]
        constv = st.enter_context(nc.sbuf_tensor("constv_sb", [128, 4], F32))
        nmax = st.enter_context(nc.sbuf_tensor("nmax", [128, 1], F32))
        mstg = [None, None]
        rs_c = st.enter_context(nc.sbuf_tensor("rs_c", [128, 1], F32))
        rr_all = st.enter_context(nc.sbuf_tensor("rr_all", [128, 16], F32))
        PS = [st.enter_context(nc.psum_tensor(f"ps{i}", [128, 512], F32))
              for i in range(8)]
        for s in sem_names:
            ctx.S[s] = st.enter_context(nc.semaphore(s))

        def bf(a, col, n):
            return a[:, col:col + n]

        def f32v(a, col, n):
            return a[:, col:col + 2 * n].bitcast(F32)

        def i32v(a, col, n):
            return a[:, col:col + 2 * n].bitcast(I32)

        mstg[0] = xres[0][:, 0:256].bitcast(BF16)
        mstg[1] = xres[1][:, 0:256].bitcast(BF16)

        nc.const_aps.register(F32, -7.5, constv[:, 0:1])
        nc.const_aps.register(F32, EPS, constv[:, 1:2])
        nc.const_aps.register(F32, -3.75, constv[:, 2:3])
        nc.const_aps.register(F32, 0.5, constv[:, 3:4])

        def ln_views(arena):
            RB = 24576
            return {
                "xg": lambda c: f32v(arena, c * 1024, 512),
                "x1g": lambda c: bf(arena, 16384 + c * 512, 512),
                "row_m": f32v(arena, RB, 512)[0:1, :],
                "row_v": f32v(arena, RB + 1024, 512)[0:1, :],
                "row_r": f32v(arena, RB + 2048, 512)[0:1, :],
                "row_mr": f32v(arena, RB + 3072, 512)[0:1, :],
                "xb_s": lambda i: bf(arena, RB + 4096 + i * 512, 512),
                "sq_s": lambda i: bf(arena, RB + 5120 + i * 512, 512),
                "x1st": lambda i: bf(arena, 16384 + i * 512, 512),
            }
        LNV = {"p1": ln_views(R2), "p2": ln_views(R3), "p3": ln_views(R2)}
        scst = f32v(R2, 0, 2048)
        smx = lambda i: f32v(R2, 4096 + i * 4096, 2048)
        wvh = bf(R2, 16384, 16384)
        wo_sb = lambda e: bf(R2, e * 2048, 2048)

        upb = lambda s: bf(R3, s * 8192, 8192)
        dnb = lambda s: bf(R3, 16384 + s * 8192, 8192)
        dq_qi = lambda i: i32v(R3, i * 1024, 512)
        dq_qf = lambda i: f32v(R3, 2048 + i * 1024, 512)
        dq_ob = lambda i: bf(R3, 6144 + i * 512, 512)
        dq_spl = lambda c: f32v(R3, 8192 + c * 4096, 2048)
        dq_bd = f32v(R3, 16384, 2048)[0:16, :]
        lAu = f32v(R3, 7168, 256)[0:16, :]
        lAd = f32v(R3, 20480, 1024)[0:16, :]
        lB = [f32v(R3, 22528 + i * 1024, 512)[0:16, :] for i in range(2)]
        x1v = lambda i: bf(R3, i * 8192, 8192)
        wsm = lambda e: bf(R3, ((e + 8) % 16) * 2048, 2048)
        cv_in = lambda i: f32v(R1, i * 4096, 2048)
        cv_ob = lambda i: bf(R1, 16384 + i * 2048, 2048)

        def wrow(kind, c):
            return 1024 * (c // 2) + 256 * kind + 128 * (c % 2)

        GROUPS8 = [list(range(8))]
        GROUPS4 = [[0, 1, 2, 3], [4, 5, 6, 7]]

        def p0(en, x):
            s_, g_, t_, a_, v_ = (en == 'S'), (en == 'G'), (en == 'T'), (en == 'A'), (en == 'V')
            if g_:
                x.dma(onec[:, :], dIN["ones_col"].ap(), "aCON")
                x.dma(oner[:, :], dIN["ones_row"].ap(), "aCON")
                x.dma(exp2[:, :], dIN["exp2"].ap(), "aCON")
                x.dma(exp2h[:, :], dIN["exp2h"].ap(), "aCON")
                x.dma(constv[:, :], dIN["constv"].ap(), "aCON")
                x.dma(bupt[:, :], dIN["b_up_t"].ap(), "aCON")
                x.dma(bdnt[:, :], dIN["b_dn_t"].ap(), "aCON")
                x.dma(lAu[:, :], dIN["AupT_sl"].ap(), "aCON")
                x.dma(lAd[:, :], dIN["AdnT_sl"].ap(), "aCON")
                x.dma(dq_bd, dIN["Bdn"].ap(), "aCON", mark="p0.consts")
            if v_:
                x.wait("aCON", "p0.consts")
                x.op("tensor_scalar_mul", bdnh[:, :], bdnt[:, :], 0.5)
                x.op("tensor_scalar_mul", dq_bd, dq_bd, 0.5,
                     sem="DVE", mark="p0.bdnh")

            for i in range(8):
                sl = i % 2
                if s_:
                    if i >= 2:
                        x.wait("DVE", f"p0.cv.{i - 2}")
                    x.dma(cv_in(sl), dIN["wqkvo_sl"].ap()[i * 128:(i + 1) * 128, :],
                          f"wCV{sl}", mark=f"p0.cvl.{i}")
                if v_:
                    x.wait(f"wCV{sl}", f"p0.cvl.{i}")
                    if i >= 2:
                        x.wait(f"aCVW{sl}", f"p0.cvw.{i - 2}")
                    if i < 2:
                        x.op("tensor_scalar_mul", cv_ob(sl), cv_in(sl), 0.125,
                             sem="DVE", mark=f"p0.cv.{i}")
                    else:
                        x.op("tensor_copy", cv_ob(sl), cv_in(sl),
                             sem="DVE", mark=f"p0.cv.{i}")
                if g_:
                    x.wait("DVE", f"p0.cv.{i}")
                    x.dma(wqkvo_c.ap()[i * 128:(i + 1) * 128, :], cv_ob(sl),
                          f"aCVW{sl}", mark=f"p0.cvw.{i}")
            if g_:
                x.wait("aCVW0", "p0.cvw.6")
                x.wait("aCVW1", "p0.cvw.7")
                x.cc("AllGather", ALU.bypass, GROUPS8,
                     wqkvo_c.ap().opt(), wqkvo_g.ap().opt(), "ccQ", mark="p0.agq")

            nblk = 0
            for hr in range(8):
                for c in range(2):
                    if s_ and hr % 2 == 0:
                        if nblk >= 2:
                            x.wait("DVE", f"p0.up.mul.{nblk - 2}")
                        q = hr // 2
                        x.dma(dq_spl(c), dIN["s_up_p"].ap()[c, q, :, :], f"wSP{c}",
                              mark=f"p0.up.sp.{c}.{hr // 2}")
                    for hb in (2 * hr, 2 * hr + 1):
                        bsl = nblk % 2
                        if s_:
                            if nblk >= 2:
                                x.wait("DVE", f"p0.up.mul.{nblk - 2}")
                            x.dma(dq_qi(bsl),
                                  dIN["wupqT"].ap()[c * 128:(c + 1) * 128,
                                                    hb * 512:(hb + 1) * 512], f"wUQ{bsl}")
                            x.dma(lB[bsl][:, :],
                                  dIN["Bup"].ap()[:, hb * 512:(hb + 1) * 512],
                                  f"wUQ{bsl}", mark=f"p0.up.qi.{nblk}")
                        if t_:
                            x.wait(f"wUQ{bsl}", f"p0.up.qi.{nblk}")
                            if nblk == 0:
                                x.wait("aCON", "p0.consts")
                            if nblk >= 2:
                                x.wait("DVE", f"p0.up.mul.{nblk - 2}")
                            x.mm(PS[7 if nblk % 2 == 0 else 5][:, :],
                                 lAu[:, c * 128:(c + 1) * 128],
                                 lB[bsl][:, :], True, True,
                                 sem="PE", mark=f"p0.up.pe.{nblk}")
                        if a_:
                            x.wait(f"wUQ{bsl}", f"p0.up.qi.{nblk}")
                            if nblk == 0:
                                x.wait("aCON", "p0.consts")
                            if nblk >= 2:
                                x.wait("DVE", f"p0.up.mul.{nblk - 2}")
                            x.op("activation", dq_qf(bsl), dq_qi(bsl), AF.Copy,
                                 bias=-7.5, sem="ACT", mark=f"p0.up.cv.{nblk}")
                        if v_:
                            x.wait("ACT", f"p0.up.cv.{nblk}")
                            x.wait("PE", f"p0.up.pe.{nblk}")
                            x.wait(f"wSP{c}", f"p0.up.sp.{c}.{hr // 2}")
                            if nblk >= 2:
                                x.wait(f"aWR{bsl}", f"p0.up.wr.{nblk - 2}")
                            off = (hb % 4) * 512
                            x.op("tensor_mul", dq_qf(bsl), dq_qf(bsl),
                                 dq_spl(c)[:, off:off + 512])
                            x.op("tensor_add", dq_ob(bsl), dq_qf(bsl),
                                 PS[7 if nblk % 2 == 0 else 5][:, :],
                                 sem="DVE", mark=f"p0.up.mul.{nblk}")
                        if g_:
                            x.wait("DVE", f"p0.up.mul.{nblk}")
                            x.dma(wup_c[hr].ap()[c * 128:(c + 1) * 128,
                                                 (hb % 2) * 512:(hb % 2) * 512 + 512],
                                  dq_ob(bsl), f"aWR{bsl}", mark=f"p0.up.wr.{nblk}")
                        nblk += 1
                if g_:
                    x.wait("aWR0", f"p0.up.wr.{nblk - 2}")
                    x.wait("aWR1", f"p0.up.wr.{nblk - 1}")
                    x.cc("AllGather", ALU.bypass, GROUPS8,
                         wup_c[hr].ap().opt(), wup_g[hr].ap().opt(),
                         f"ccU{hr}", mark=f"p0.agu.{hr}")

            nb = 0
            for j in range(8):
                if s_:
                    if j >= 2:
                        x.wait("DVE", f"p0.dn.mul.{nb - 5}")
                    else:
                        x.wait("DVE", "p0.up.mul.31")
                    x.dma(dq_spl(j % 2), dIN["s_dn_p"].ap()[j, :, :], "wSPD",
                          mark=f"p0.dn.sp.{j}")
                for db in range(4):
                    bsl = nb % 2
                    if s_:
                        if nb >= 2:
                            x.wait("DVE", f"p0.dn.mul.{nb - 2}")
                        x.dma(dq_qi(bsl),
                              dIN["wdnqT"].ap()[j * 128:(j + 1) * 128,
                                                db * 512:(db + 1) * 512],
                              f"wUQ{bsl}", mark=f"p0.dn.qi.{nb}")
                    if t_:
                        x.wait(f"wUQ{bsl}", f"p0.dn.qi.{nb}")
                        if nb == 0:
                            x.wait("DVE", "p0.bdnh")
                            x.wait("DVE", "p0.up.mul.31")
                        if nb >= 2:
                            x.wait("DVE", f"p0.dn.mul.{nb - 2}")
                        off = db * 512
                        x.mm(PS[7 if nb % 2 == 0 else 5][:, :],
                             lAd[:, j * 128:(j + 1) * 128],
                             dq_bd[:, off:off + 512], True, True,
                             sem="PE", mark=f"p0.dn.pe.{nb}")
                    if a_:
                        x.wait(f"wUQ{bsl}", f"p0.dn.qi.{nb}")
                        if nb >= 2:
                            x.wait("DVE", f"p0.dn.mul.{nb - 2}")
                        x.op("activation", dq_qf(bsl), dq_qi(bsl), AF.Copy,
                             bias=-3.75, scale=0.5,
                             sem="ACT", mark=f"p0.dn.cv.{nb}")
                    if v_:
                        x.wait("ACT", f"p0.dn.cv.{nb}")
                        x.wait("PE", f"p0.dn.pe.{nb}")
                        x.wait("wSPD", f"p0.dn.sp.{j}")
                        if nb >= 2:
                            x.wait(f"aWR{bsl}", f"p0.dn.wr.{nb - 2}")
                        else:
                            x.wait(f"aWR{bsl}", f"p0.up.wr.{30 + nb}")
                        off = db * 512
                        x.op("tensor_mul", dq_qf(bsl), dq_qf(bsl),
                             dq_spl(j % 2)[:, off:off + 512])
                        x.op("tensor_add", dq_ob(bsl), dq_qf(bsl),
                             PS[7 if nb % 2 == 0 else 5][:, :],
                             sem="DVE", mark=f"p0.dn.mul.{nb}")
                    if g_:
                        x.wait("DVE", f"p0.dn.mul.{nb}")
                        x.dma(wdn_c[db].ap()[j * 128:(j + 1) * 128, :],
                              dq_ob(bsl), f"aWR{bsl}", mark=f"p0.dn.wr.{nb}")
                    nb += 1
            if g_:
                x.wait("aWR0", f"p0.dn.wr.{nb - 2}")
                x.wait("aWR1", f"p0.dn.wr.{nb - 1}")
                for qd in range(4):
                    x.cc("AllGather", ALU.bypass, GROUPS8,
                         wdn_c[qd].ap().opt(), wdn_g[qd].ap().opt(),
                         f"ccD{qd}", mark=f"p0.agd.{qd}")

        def pred(ph, what):
            tbl = {
                ("p1", "xgfree"): [],
                ("p2", "xgfree"): [("PE", "p1.g3.dn.15")],
                ("p3", "xgfree"): [("PE", "p2.M.63"), ("DVE", "p2.out2.done")],
                ("p1", "xdata"): [],
                ("p2", "xdata"): [("aYW0", "p1.g3.yw.14"), ("aYW1", "p1.g3.yw.15")],
                ("p3", "xdata"): [("aYW0", "p2.o.yw.62"), ("aYW1", "p2.o.yw.63")],
                ("p1", "ps45"): [("DVE", "p0.dn.mul.31")],
                ("p2", "ps45"): [("DVE", "p1.ln3.var")],
                ("p3", "ps45"): [("DVE", "p2.ln3.var")],
                ("p1", "ps67"): [("DVE", "p0.dn.mul.31")],
                ("p2", "ps67"): [("DVE", "p1.ln3.nrm.15")],
                ("p3", "ps67"): [("DVE", "p2.ln3.nrm.15")],
            }
            return tbl[(ph, what)]

        def ln(en, x, ph, g, src, spill_x1, part="all", inline_ta=False):
            g_, t_, a_, v_ = (en == 'G'), (en == 'T'), (en == 'A'), (en == 'V')
            pre = f"{ph}.ln{g}"
            V = LNV[ph]
            xg, x1g, x1st = V["xg"], V["x1g"], V["x1st"]
            row_m, row_v, row_r, row_mr = (V["row_m"], V["row_v"],
                                           V["row_r"], V["row_mr"])
            xb_s, sq_s = V["xb_s"], V["sq_s"]
            do_x = part in ("all", "x")
            do_pre = part in ("all", "pre")
            do_planes = part in ("all", "planes")
            do_norm = part in ("all", "norm")
            if g_ and do_x:
                if g > 0:
                    x.wait("DVE", f"{ph}.ln{g - 1}.nrm.15")
                else:
                    for p in pred(ph, "xgfree") + pred(ph, "xdata"):
                        x.wait(*p)
                arena = R3 if ph == "p2" else R2
                x.dma(arena[:, 0:16384].bitcast(F32).rearrange(
                          "p (c t) -> p c t", c=16),
                      src.ap().rearrange("(c p) t -> p c t", p=128)[
                          :, :, g * TG:(g + 1) * TG],
                      "aXG", mark=f"{pre}.x")
            for c in (range(NCF) if do_pre else ()):
                sl = c % 2
                if a_ and not inline_ta:
                    x.wait("aXG", f"{pre}.x")
                    if c >= 2:
                        x.wait("PE", f"{pre}.st.{c - 2}")
                    x.op("activation", sq_s(sl), xg(c), AF.Square,
                         sem="ACT", mark=f"{pre}.sq.{c}")
                if v_:
                    x.wait("aXG", f"{pre}.x")
                    if c >= 2:
                        x.wait("PE", f"{pre}.st.{c - 2}")
                    x.op("tensor_copy", xb_s(sl), xg(c),
                         sem="DVE", mark=f"{pre}.xb.{c}")
                if t_ and not inline_ta:
                    x.wait("ACT", f"{pre}.sq.{c}")
                    x.wait("DVE", f"{pre}.xb.{c}")
                    if c == 0:
                        if g > 0:
                            x.wait("DVE", f"{ph}.ln{g - 1}.var")
                        else:
                            for p in pred(ph, "ps45"):
                                x.wait(*p)
                    x.mm(PS[4][0:1, :], onec[:, :], xb_s(sl), c == 0, c == NCF - 1)
                    x.mm(PS[5][0:1, :], onec[:, :], sq_s(sl), c == 0, c == NCF - 1,
                         sem="PE", mark=f"{pre}.st.{c}")
            if v_ and do_pre:
                x.wait("PE", f"{pre}.st.15")
                x.op("tensor_scalar_mul", row_m, PS[4][0:1, :], 1.0 / F)
                x.op("tensor_scalar_mul", row_v, PS[5][0:1, :], 1.0 / F)
                x.op("tensor_mul", row_r, row_m, row_m)
                x.op("tensor_sub", row_v, row_v, row_r,
                     sem="DVE", mark=f"{pre}.var")
            if a_ and do_pre:
                x.wait("DVE", f"{pre}.var")
                x.op("activation", row_v, row_v, AF.Sqrt, bias=EPS,
                     sem="ACT", mark=f"{pre}.sqrt")
            if v_ and do_pre:
                x.wait("ACT", f"{pre}.sqrt")
                x.op("reciprocal", row_r, row_v)
                x.op("tensor_mul", row_mr, row_m, row_r,
                     sem="DVE", mark=f"{pre}.rows")
            if t_ and do_planes:
                x.wait("DVE", f"{pre}.rows")
                if g > 0:
                    x.wait("DVE", f"{ph}.ln{g - 1}.nrm.15")
                else:
                    for p in pred(ph, "ps67"):
                        x.wait(*p)
                x.mm(PS[6][:, :], oner[:, :], row_r, True, True)
                x.mm(PS[7][:, :], oner[:, :], row_mr, True, True,
                     sem="PE", mark=f"{pre}.pl")
            for c in (range(NCF) if do_norm else ()):
                if v_:
                    x.wait("PE", f"{pre}.pl")
                    if spill_x1 and c >= 2:
                        x.wait(f"aX1W{c % 2}", f"{pre}.x1w.{c - 2}")
                    elif spill_x1 and g > 0:
                        x.wait(f"aX1W{c % 2}", f"{ph}.ln{g - 1}.x1w.{14 + c}")
                    if (not spill_x1) and g == 0 and c == 0 and ph == "p3":
                        x.wait("aX1W0", "p2.ln3.x1w.14")
                        x.wait("aX1W1", "p2.ln3.x1w.15")
                    x.op("tensor_mul", xg(c), xg(c), PS[6][:, :])
                    dst = x1st(c % 2) if spill_x1 else x1g(c)
                    x.op("tensor_sub", dst, xg(c), PS[7][:, :],
                         sem="DVE", mark=f"{pre}.nrm.{c}")
                if spill_x1 and g_:
                    x.wait("DVE", f"{pre}.nrm.{c}")
                    x.dma(x1_d.ap()[c * 128:(c + 1) * 128, g * TG:(g + 1) * TG],
                          x1st(c % 2), f"aX1W{c % 2}", mark=f"{pre}.x1w.{c}")

        def ffn(en, x, ph, src, dst):
            s_, g_, t_, a_, v_ = (en == 'S'), (en == 'G'), (en == 'T'), (en == 'A'), (en == 'V')
            is_p1 = (ph == "p1")
            x1g = LNV[ph]["x1g"]
            ln(en, x, ph, 0, src, spill_x1=False)
            for g in range(NG):
                if g < 3:
                    ln(en, x, ph, g + 1, src, spill_x1=False, part="x")
                for hr in range(16):
                    sl = hr % 2
                    if s_:
                        if hr >= 2:
                            x.wait("PE", f"{ph}.g{g}.up.r{hr - 2}.done")
                        elif g > 0:
                            x.wait("PE", f"{ph}.g{g - 1}.up.r{14 + hr}.done")
                        elif is_p1:
                            x.wait("aWR0", "p0.dn.wr.30")
                            x.wait("aWR1", "p0.dn.wr.31")
                        else:
                            x.wait("PE", "p2.M.63")
                        x.wait(f"ccU{hr // 2}", f"p0.agu.{hr // 2}")
                        for c in range(NCF):
                            mk = f"{ph}.g{g}.upl.{hr}" if c == NCF - 1 else None
                            x.dma(upb(sl)[:, c * 512:(c + 1) * 512],
                                  wup_g[hr // 2].ap()[
                                      c * 128:(c + 1) * 128,
                                      (hr % 2) * 512:(hr % 2) * 512 + 512],
                                  f"wUP{sl}", mark=mk)
                    if t_:
                        x.wait(f"wUP{hr % 2}", f"{ph}.g{g}.upl.{hr}")
                        if hr == 0:
                            x.wait("DVE", f"{ph}.ln{g}.nrm.15")
                        for ht in range(4):
                            htg = hr * 4 + ht
                            bank = htg % 4
                            if htg >= 4:
                                x.wait("ACT", f"{ph}.g{g}.upe.{htg - 4}")
                            elif g > 0:
                                x.wait("DVE", f"{ph}.g{g - 1}.dne.{12 + htg}")
                            elif ph == "p3":
                                x.wait("DVE", "p2.out2.done")
                            for c in range(NCF):
                                last = (c == NCF - 1)
                                mk = None
                                if last:
                                    mk = (f"{ph}.g{g}.up.r{hr}.done" if ht == 3
                                          else f"{ph}.g{g}.up.{htg}")
                                x.mm(PS[bank][:, :],
                                     upb(sl)[:, c * 512 + ht * 128:
                                             c * 512 + ht * 128 + 128],
                                     x1g(c), c == 0, last,
                                     sem="PE" if last else None, mark=mk)
                        if g < 3 and hr >= 8:
                            nln = f"{ph}.ln{g + 1}"
                            nxb = LNV[ph]["xb_s"]
                            nsq = LNV[ph]["sq_s"]
                            for c in (2 * (hr - 8), 2 * (hr - 8) + 1):
                                x.wait("ACT", f"{nln}.sq.{c}")
                                x.wait("DVE", f"{nln}.xb.{c}")
                                if c == 0:
                                    x.wait("DVE", f"{ph}.ln{g}.var")
                                x.mm(PS[4][0:1, :], onec[:, :], nxb(c % 2),
                                     c == 0, c == NCF - 1)
                                x.mm(PS[5][0:1, :], onec[:, :], nsq(c % 2),
                                     c == 0, c == NCF - 1,
                                     sem="PE", mark=f"{nln}.st.{c}")
                    if a_:
                        for ht in range(4):
                            htg = hr * 4 + ht
                            bank = htg % 4
                            mk = (f"{ph}.g{g}.up.r{hr}.done" if ht == 3
                                  else f"{ph}.g{g}.up.{htg}")
                            x.wait("PE", mk)
                            if g == 0 and htg == 0:
                                if is_p1:
                                    x.wait("aCVW0", "p0.cvw.6")
                                    x.wait("aCVW1", "p0.cvw.7")
                                else:
                                    x.wait("PE", "p2.o.15.done")
                            x.op("activation", bf(R1, htg * 512, 512),
                                 PS[bank][:, :], AF.Relu,
                                 bias=bupt[:, htg:htg + 1],
                                 sem="ACT", mark=f"{ph}.g{g}.upe.{htg}")
                            if g < 3 and htg >= 32 and htg % 2 == 0:
                                c = (htg - 32) // 2
                                nln = f"{ph}.ln{g + 1}"
                                nxg = LNV[ph]["xg"]
                                nsq = LNV[ph]["sq_s"]
                                x.wait("aXG", f"{nln}.x")
                                if c >= 2:
                                    x.wait("PE", f"{nln}.st.{c - 2}")
                                x.op("activation", nsq(c % 2), nxg(c), AF.Square,
                                     sem="ACT", mark=f"{nln}.sq.{c}")
                if g < 3:
                    ln(en, x, ph, g + 1, src, spill_x1=False, part="pre",
                       inline_ta=True)
                    ln(en, x, ph, g + 1, src, spill_x1=False, part="planes")
                    ln(en, x, ph, g + 1, src, spill_x1=False, part="norm")
                for ss in range(16):
                    sl = ss % 2
                    qd, qc = ss // 4, ss % 4
                    if s_:
                        if ss >= 2:
                            x.wait("PE", f"{ph}.g{g}.dn.{ss - 2}")
                        elif g > 0:
                            x.wait("PE", f"{ph}.g{g - 1}.dn.{14 + ss}")
                        elif is_p1:
                            x.wait("aWR0", "p0.dn.wr.30")
                            x.wait("aWR1", "p0.dn.wr.31")
                        else:
                            x.wait("PE", "p2.M.63")
                        x.wait(f"ccD{qd}", f"p0.agd.{qd}")
                        x.dma(dnb(sl).rearrange("p (h c) -> p h c", h=64),
                              wdn_g[qd].ap().rearrange("(h p) c -> p h c", p=128)[
                                  :, :, qc * 128:(qc + 1) * 128],
                              f"wDN{sl}", mark=f"{ph}.g{g}.dnl.{ss}")
                    if g_:
                        if ss >= 2:
                            x.wait(f"aYW{sl}", f"{ph}.g{g}.yw.{ss - 2}")
                        elif g > 0:
                            x.wait(f"aYW{sl}", f"{ph}.g{g - 1}.yw.{14 + ss}")
                        elif ph == "p3":
                            x.wait(f"aYW{sl}", f"p2.o.yw.{62 + ss}")
                        x.dma(xres[sl][:, :],
                              src.ap()[ss * 128:(ss + 1) * 128,
                                       g * TG:(g + 1) * TG],
                              f"aXR{sl}", mark=f"{ph}.g{g}.xr.{ss}")
                    if t_:
                        bank = ss % 4
                        x.wait(f"wDN{ss % 2}", f"{ph}.g{g}.dnl.{ss}")
                        if ss == 0:
                            x.wait("ACT", f"{ph}.g{g}.upe.63")
                        if ss >= 4:
                            x.wait("DVE", f"{ph}.g{g}.dne.{ss - 4}")
                        elif g > 0:
                            x.wait("DVE", f"{ph}.g{g - 1}.dne.{12 + ss}")
                        for h in range(64):
                            last = (h == 63)
                            x.mm(PS[bank][:, :],
                                 dnb(sl)[:, h * 128:(h + 1) * 128],
                                 bf(R1, h * 512, 512), h == 0, last,
                                 sem="PE" if last else None,
                                 mark=f"{ph}.g{g}.dn.{ss}" if last else None)
                    if v_:
                        bank = ss % 4
                        x.wait("PE", f"{ph}.g{g}.dn.{ss}")
                        x.wait(f"aXR{ss % 2}", f"{ph}.g{g}.xr.{ss}")
                        if ss >= 2:
                            x.wait(f"aYW{ss % 2}", f"{ph}.g{g}.yw.{ss - 2}")
                        elif g > 0:
                            x.wait(f"aYW{ss % 2}", f"{ph}.g{g - 1}.yw.{14 + ss}")
                        x.op("scalar_tensor_tensor", xres[sl][:, :],
                             PS[bank][:, :], bdnh[:, ss:ss + 1], xres[sl][:, :],
                             ALU.add, ALU.add,
                             sem="DVE", mark=f"{ph}.g{g}.dne.{ss}")
                    if g_:
                        x.wait("DVE", f"{ph}.g{g}.dne.{ss}")
                        x.dma(dst.ap()[ss * 128:(ss + 1) * 128,
                                       g * TG:(g + 1) * TG],
                              xres[sl][:, :], f"aYW{sl}", mark=f"{ph}.g{g}.yw.{ss}")
        def attn(en, x):
            s_, g_, t_, a_, v_ = (en == 'S'), (en == 'G'), (en == 'T'), (en == 'A'), (en == 'V')
            if s_:
                x.wait("PE", "p1.g3.up.r15.done")
                x.wait("DVE", "p1.ln3.nrm.15")
                for c in range(NCF):
                    x.dma(bf(R2, c * 2048, 2048),
                          wqkvo_g.ap()[wrow(0, c):wrow(0, c) + 128, :],
                          "wW", mark="p2.q.wl" if c == NCF - 1 else None)
            def q_or_k(wn, arena, st_range):
                for st_ in st_range:
                    sl = st_ % 2
                    if g_:
                        if st_ >= 2:
                            x.wait("PE", f"p2.{wn}.{st_ - 2}.done")
                        elif wn == "k":
                            x.wait("PE", f"p2.q.{14 + st_}.done")
                        x.wait("aX1W0", f"p2.ln{st_ // 4}.x1w.14")
                        x.wait("aX1W1", f"p2.ln{st_ // 4}.x1w.15")
                        x.dma(x1tb[sl][:, :].rearrange("p (c s) -> p c s", c=16),
                              x1_d.ap().rearrange("(c p) s -> p c s", p=128)[
                                  :, :, st_ * 128:(st_ + 1) * 128],
                              f"aX1R{sl}", mark=f"p2.{wn}.x1.{st_}")
                    if t_:
                        if st_ == 0:
                            x.wait("wW", f"p2.{wn}.wl")
                            if wn == "q":
                                x.wait("ACT", "p1.g3.upe.63")
                                x.wait("DVE", "p1.g3.dne.15")
                        x.wait(f"aX1R{sl}", f"p2.{wn}.x1.{st_}")
                        x.wait("aX1W0", f"p2.ln{st_ // 4}.x1w.14")
                        x.wait("aX1W1", f"p2.ln{st_ // 4}.x1w.15")
                        for eg in range(4):
                            n = st_ * 4 + eg
                            bank = n % 4
                            if n >= 4:
                                x.wait("ACT", f"p2.{wn}.ev.{n - 4}")
                            elif wn == "k":
                                x.wait("ACT", f"p2.q.ev.{60 + n}")
                            for c in range(NCF):
                                last = (c == NCF - 1)
                                mk = None
                                if last:
                                    mk = (f"p2.{wn}.{st_}.done" if eg == 3
                                          else f"p2.{wn}.{st_}.{eg}")
                                x.mm(PS[bank][:, :],
                                     x1tb[sl][:, c * 128:(c + 1) * 128],
                                     bf(R2, c * 2048 + eg * 512, 512),
                                     c == 0, last,
                                     sem="PE" if last else None, mark=mk)
                    if a_:
                        for eg in range(4):
                            n = st_ * 4 + eg
                            bank = n % 4
                            mk = (f"p2.{wn}.{st_}.done" if eg == 3
                                  else f"p2.{wn}.{st_}.{eg}")
                            x.wait("PE", mk)
                            if wn == "k" and n == 0:
                                x.wait("PE", "p1.g3.dn.15")
                                x.wait("aX1W0", "p2.ln3.x1w.14")
                                x.wait("aX1W1", "p2.ln3.x1w.15")
                            x.op("activation",
                                 arena[:, st_ * 2048 + eg * 512:
                                       st_ * 2048 + eg * 512 + 512],
                                 PS[bank][:, :], AF.Copy,
                                 sem="ACT", mark=f"p2.{wn}.ev.{n}")

            for g in range(NG):
                ln(en, x, "p2", g, x_a, spill_x1=True)
                q_or_k("q", R1, range(4 * g, 4 * g + 4))
            if s_:
                x.wait("PE", "p2.q.15.done")
                for c in range(NCF):
                    x.dma(bf(R2, c * 2048, 2048),
                          wqkvo_g.ap()[wrow(1, c):wrow(1, c) + 128, :],
                          "wW", mark="p2.k.wl" if c == NCF - 1 else None)
            q_or_k("k", R3, range(16))
            if s_:
                x.wait("PE", "p2.k.15.done")
                for c in range(NCF):
                    x.dma(wvh[:, c * 1024:(c + 1) * 1024],
                          wqkvo_g.ap()[wrow(2, c):wrow(2, c) + 128, 0:1024],
                          "wW", mark="p2.v.wl.0" if c == NCF - 1 else None)
            for et in range(16):
                for fg in range(4):
                    n = et * 4 + fg
                    bank = n % 4
                    if t_:
                        if n == 0:
                            x.wait("ACT", "p2.k.ev.63")
                        if n >= 4:
                            x.wait("ACT", f"p2.sc.ev.{n - 4}")
                        for st_ in range(16):
                            last = (st_ == 15)
                            x.mm(PS[bank][:, :],
                                 R1[:, st_ * 2048 + et * 128:
                                    st_ * 2048 + et * 128 + 128],
                                 R3[:, st_ * 2048 + fg * 512:
                                    st_ * 2048 + fg * 512 + 512],
                                 st_ == 0, last,
                                 sem="PE" if last else None,
                                 mark=f"p2.sc.{n}" if last else None)
                    if a_:
                        x.wait("PE", f"p2.sc.{n}")
                        if et >= 1 and fg == 0:
                            x.wait("wSC", f"p2.scw.{et - 1}")
                        x.op("activation", scst[:, fg * 512:(fg + 1) * 512],
                             PS[bank][:, :], AF.Copy,
                             sem="ACT", mark=f"p2.sc.ev.{n}")
                if s_:
                    x.wait("ACT", f"p2.sc.ev.{et * 4 + 3}")
                    x.dma(sc_p.ap()[et * 128:(et + 1) * 128, :], scst,
                          "wSC", mark=f"p2.scw.{et}")
                if g_:
                    if et % 4 == 3:
                        q = et // 4
                        x.wait("wSC", f"p2.scw.{et}")
                        x.cc("AllReduce", ALU.add, GROUPS4,
                             sc_p.ap()[q * 512:(q + 1) * 512, :].opt(),
                             sc_r.ap()[q * 512:(q + 1) * 512, :].opt(),
                             f"ccS{q}", mark=f"p2.ar.{q}")
            for half in range(2):
                if s_ and half == 1:
                    x.wait("PE", "p2.v.c03")
                    for c in range(NCF):
                        x.dma(wvh[:, c * 1024:(c + 1) * 1024],
                              wqkvo_g.ap()[wrow(2, c):wrow(2, c) + 128,
                                           1024:2048],
                              "wW", mark="p2.v.wl.1" if c == NCF - 1 else None)
                for sg in range(NG):
                    vsl = sg % 2
                    if g_:
                        if half == 0 and sg == 0:
                            x.wait("PE", "p2.sc.63")
                        nprev = half * 4 + sg - 2
                        if nprev >= 0:
                            hp, sp_ = nprev // 4, nprev % 4
                            x.wait("PE", f"p2.v.c{hp}{sp_}")
                        x.dma(x1v(vsl).rearrange("p (c s) -> p c s", c=16),
                              x1_d.ap().rearrange("(c p) s -> p c s", p=128)[
                                  :, :, sg * TG:(sg + 1) * TG],
                              f"aXV{vsl}", mark=f"p2.v.x1.{half}.{sg}")
                    if t_:
                        x.wait("wW", f"p2.v.wl.{half}")
                        x.wait(f"aXV{vsl}", f"p2.v.x1.{half}.{sg}")
                        for fi in range(8):
                            ft = half * 8 + fi
                            n = half * 32 + sg * 8 + fi
                            bank = n % 4
                            if n >= 4:
                                x.wait("ACT", f"p2.v.ev.{n - 4}")
                            elif n < 4:
                                x.wait("ACT", f"p2.sc.ev.{60 + n}")
                            for c in range(NCF):
                                last = (c == NCF - 1)
                                mk = None
                                if last:
                                    mk = (f"p2.v.c{half}{sg}" if fi == 7
                                          else f"p2.v.n.{n}")
                                x.mm(PS[bank][:, :],
                                     wvh[:, c * 1024 + fi * 128:
                                         c * 1024 + fi * 128 + 128],
                                     x1v(vsl)[:, c * 512:(c + 1) * 512],
                                     c == 0, last,
                                     sem="PE" if last else None, mark=mk)
                    if a_:
                        for fi in range(8):
                            ft = half * 8 + fi
                            n = half * 32 + sg * 8 + fi
                            bank = n % 4
                            mk = (f"p2.v.c{half}{sg}" if fi == 7
                                  else f"p2.v.n.{n}")
                            x.wait("PE", mk)
                            if n == 0:
                                x.wait("PE", "p2.sc.63")
                                x.wait("aX1W0", "p2.ln3.x1w.14")
                                x.wait("aX1W1", "p2.ln3.x1w.15")
                            x.op("activation",
                                 R1[:, ft * 2048 + sg * 512:
                                    ft * 2048 + sg * 512 + 512],
                                 PS[bank][:, :], AF.Copy,
                                 sem="ACT", mark=f"p2.v.ev.{n}")
            SM_ORDER = list(range(16))
            for i, et in enumerate(SM_ORDER):
                q = et // 4
                sl = i % 2
                if g_:
                    x.wait(f"ccS{q}", f"p2.ar.{q}")
                    if i >= 2:
                        x.wait("ACT", f"p2.sm.exp.{SM_ORDER[i - 2]}")
                    if i == 0:
                        x.wait("PE", "p2.sc.63")
                    x.dma(smx(sl), sc_r.ap()[et * 128:(et + 1) * 128, :],
                          f"aSM{sl}", mark=f"p2.sm.l.{et}")
                if v_:
                    x.wait(f"aSM{sl}", f"p2.sm.l.{et}")
                    if i > 0:
                        x.wait("ACT", f"p2.sm.exp.{SM_ORDER[i - 1]}")
                    x.op("reduce_max", nmax[:, :], smx(sl), axis=AX.X,
                         negate=True, sem="DVE", mark=f"p2.sm.mx.{et}")
                if a_:
                    x.wait("DVE", f"p2.sm.mx.{et}")
                    if i == 0:
                        x.wait("PE", "p2.sc.63")
                    if et >= 8:
                        x.wait("PE", "p2.v.c13")
                    x.op("activation", wsm(et), smx(sl), AF.Exp,
                         bias=nmax[:, :], accum_out=rs_c[:, :],
                         sem="ACT", mark=f"p2.sm.exp.{et}")
                if v_:
                    x.wait("ACT", f"p2.sm.exp.{et}")
                    x.op("reciprocal", rr_all[:, et:et + 1], rs_c[:, :],
                         sem="DVE", mark=f"p2.sm.rr.{et}")
            for e in list(range(8, 16)) + list(range(8)):
                if s_:
                    if e == 8:
                        x.wait("PE", "p2.v.c13")
                    if e == 0:
                        x.wait("ACT", "p2.sm.exp.15")
                        x.wait("wSC", "p2.scw.15")
                    x.dma(wo_sb(e),
                          wqkvo_g.ap()[wrow(3, e):wrow(3, e) + 128, :],
                          "wW", mark=f"p2.wo.l.{e}")
                if v_:
                    x.wait("wW", "p2.wo.l.7")
                    x.wait("DVE", "p2.sm.rr.15")
                    x.op("tensor_scalar_mul", wo_sb(e), wo_sb(e),
                         rr_all[:, e:e + 1],
                         sem="DVE", mark=f"p2.wo.f.{e}")
            for eo in range(4):
                for ft in range(16):
                    n = eo * 16 + ft
                    bank = n % 4
                    if t_:
                        if n == 0:
                            x.wait("ACT", "p2.sm.exp.15")
                            x.wait("DVE", "p2.wo.f.7")
                        if n >= 4:
                            x.wait("ACT", f"p2.M.ev.{n - 4}")
                        for e in range(16):
                            last = (e == 15)
                            x.mm(PS[bank][:, :],
                                 wsm(e)[:, ft * 128:ft * 128 + 128],
                                 wo_sb(e)[:, eo * 512:(eo + 1) * 512],
                                 e == 0, last,
                                 sem="PE" if last else None,
                                 mark=f"p2.M.{n}" if last else None)
                    if a_:
                        x.wait("PE", f"p2.M.{n}")
                        if n >= 2:
                            x.wait(f"aMW{n % 2}", f"p2.M.w.{n - 2}")
                        x.op("activation", mstg[n % 2][:, :], PS[bank][:, :],
                             AF.Copy, sem="ACT", mark=f"p2.M.ev.{n}")
                    if g_:
                        x.wait("ACT", f"p2.M.ev.{n}")
                        x.dma(M_d.ap()[ft * 128:(ft + 1) * 128,
                                       eo * 512:(eo + 1) * 512],
                              mstg[n % 2][:, :], f"aMW{n % 2}", mark=f"p2.M.w.{n}")
            for et in range(16):
                sl = et % 2
                if s_:
                    if et >= 2:
                        x.wait("PE", f"p2.o.{et - 2}.done")
                    else:
                        x.wait("PE", "p2.k.15.done")
                    x.wait("aMW0", f"p2.M.w.{(et // 4) * 16 + 14}")
                    x.wait("aMW1", f"p2.M.w.{(et // 4) * 16 + 15}")
                    x.dma(x1tb[sl][:, :].rearrange("p (c e) -> p c e", c=16),
                          M_d.ap().rearrange("(c p) e -> p c e", p=128)[
                              :, :, et * 128:(et + 1) * 128],
                          f"wM{sl}", mark=f"p2.o.ml.{et}")
                for sg in range(4):
                    n = et * 4 + sg
                    bank = n % 4
                    amark = "p2.out2.done" if n == 63 else f"p2.o.add.{n}"
                    if g_:
                        if n >= 2:
                            x.wait(f"aYW{n % 2}", f"p2.o.yw.{n - 2}")
                        else:
                            x.wait("aMW0", "p2.M.w.62")
                            x.wait("aMW1", "p2.M.w.63")
                            x.wait(f"aYW{n % 2}", f"p1.g3.yw.{14 + n}")
                        x.dma(xres[n % 2][:, :],
                              x_a.ap()[et * 128:(et + 1) * 128,
                                       sg * TG:(sg + 1) * TG],
                              f"aXR{n % 2}", mark=f"p2.o.xr.{n}")
                    if t_:
                        if n == 0:
                            x.wait("ACT", "p2.M.ev.63")
                        x.wait(f"wM{sl}", f"p2.o.ml.{et}")
                        if n >= 4:
                            x.wait("DVE", f"p2.o.add.{n - 4}")
                        for c in range(NCF):
                            last = (c == NCF - 1)
                            mk = None
                            if last:
                                mk = (f"p2.o.{et}.done" if sg == 3
                                      else f"p2.o.n.{n}")
                            x.mm(PS[bank][:, :],
                                 x1tb[sl][:, c * 128:(c + 1) * 128],
                                 R1[:, c * 2048 + sg * 512:
                                    c * 2048 + sg * 512 + 512],
                                 c == 0, last,
                                 sem="PE" if last else None, mark=mk)
                    if v_:
                        mk = f"p2.o.{et}.done" if sg == 3 else f"p2.o.n.{n}"
                        x.wait("PE", mk)
                        x.wait(f"aXR{n % 2}", f"p2.o.xr.{n}")
                        if n >= 2:
                            x.wait(f"aYW{n % 2}", f"p2.o.yw.{n - 2}")
                        x.op("tensor_add", xres[n % 2][:, :], xres[n % 2][:, :],
                             PS[bank][:, :], sem="DVE", mark=amark)
                    if g_:
                        x.wait("DVE", amark)
                        x.dma(x_b.ap()[et * 128:(et + 1) * 128,
                                       sg * TG:(sg + 1) * TG],
                              xres[n % 2][:, :], f"aYW{n % 2}", mark=f"p2.o.yw.{n}")

        def emit_all(en, eng):
            x = E(ctx, eng, en)
            p0(en, x)
            if n_phases >= 1:
                ffn(en, x, "p1", dIN["x_in"], x_a if n_phases > 1 else y_out)
                if ctx.dry and en == 'T':
                    ctx.plan["p1.x1.dead"] = ctx.plan["p1.g3.up.r15.done"]
            if n_phases >= 2:
                attn(en, x)
                if n_phases == 2 and en == 'G':
                    x.wait("aYW0", "p2.o.yw.62")
                    x.wait("aYW1", "p2.o.yw.63")
                    x.dma(y_out.ap(), x_b.ap(), "aYW0", mark="fin.copy")
            if n_phases >= 3:
                ffn(en, x, "p3", x_b, y_out)
            if en == 'G':
                for s in W_SEMS + A_SEMS:
                    x.wait_total(s)

        ctx.dry = True
        for en in ("S", "G", "T", "A", "V"):
            emit_all(en, None)
        ctx.totals = dict(ctx.sems)

        ctx.dry = False
        with nc.Block() as block:
            @block.sync
            def _(sync):
                ctx.sems = {}
                emit_all("S", sync)

            @block.gpsimd
            def _(gpsimd):
                ctx.sems = {}
                emit_all("G", gpsimd)

            @block.tensor
            def _(tensor):
                ctx.sems = {}
                emit_all("T", tensor)

            @block.scalar
            def _(scalar):
                ctx.sems = {}
                emit_all("A", scalar)

            @block.vector
            def _(vector):
                ctx.sems = {}
                emit_all("V", vector)

    return nc


def make_inputs(L0_attn_x, L0_attn_Wq, L0_attn_Wk, L0_attn_Wv, L0_attn_Wo,
                L0_ffn_w_up_q, L0_ffn_w_up_scale, L0_ffn_b_up,
                L0_ffn_w_up_lora_a, L0_ffn_w_up_lora_b,
                L0_ffn_w_down_q, L0_ffn_w_down_scale, L0_ffn_b_down,
                L0_ffn_w_down_lora_a, L0_ffn_w_down_lora_b):
    C = np.ascontiguousarray
    x = np.asarray(L0_attn_x)
    wupqT = np.asarray(L0_ffn_w_up_q).T
    s_upT = np.asarray(L0_ffn_w_up_scale).T
    wdnqT = np.asarray(L0_ffn_w_down_q).T
    s_dnT = np.asarray(L0_ffn_w_down_scale).T
    Wq = np.asarray(L0_attn_Wq); Wk = np.asarray(L0_attn_Wk)
    Wv = np.asarray(L0_attn_Wv); Wo = np.asarray(L0_attn_Wo)
    b_up = np.asarray(L0_ffn_b_up); b_dn = np.asarray(L0_ffn_b_down)
    AupT = np.asarray(L0_ffn_w_up_lora_a).T
    Bup = np.asarray(L0_ffn_w_up_lora_b)
    AdnT = np.asarray(L0_ffn_w_down_lora_a).T
    Bdn = np.asarray(L0_ffn_w_down_lora_b)
    b_up_t = C(b_up.reshape(64, 128).T).astype(np.float32)
    b_dn_t = C(b_dn.reshape(16, 128).T).astype(np.float32)
    ones_col = np.ones((128, 1), BF)
    ones_row = np.ones((1, 128), np.float32)
    exp2 = np.zeros((2, 128), np.float32)
    exp2[0, :64] = 1.0
    exp2[1, 64:] = 1.0
    exp2h = exp2 * 0.5
    constv_np = np.zeros((128, 4), np.float32)
    constv_np[:, 0] = -7.5
    constv_np[:, 1] = EPS
    constv_np[:, 2] = -3.75
    constv_np[:, 3] = 0.5

    in_maps = []
    for c in range(8):
        b, sl = c // 4, c % 4
        x_fm = C(x[b, sl * T:(sl + 1) * T, :].T).astype(np.float32)
        sp = np.zeros((2, 4, 128, 2048), np.float32)
        for ct in range(2):
            g0 = 4 * c + 2 * ct
            for q in range(4):
                sp[ct, q, 0:64, :] = s_upT[g0, q * 2048:(q + 1) * 2048]
                sp[ct, q, 64:128, :] = s_upT[g0 + 1, q * 2048:(q + 1) * 2048]
        sdp = np.zeros((8, 128, F), np.float32)
        for j in range(8):
            g0 = (1024 * c + 128 * j) // 64
            sdp[j, 0:64, :] = s_dnT[g0, :]
            sdp[j, 64:128, :] = s_dnT[g0 + 1, :]
        wqkvo_sl = np.concatenate([
            C(Wq[256 * c:256 * (c + 1), :]),
            C(Wk[256 * c:256 * (c + 1), :]),
            C(Wv[256 * c:256 * (c + 1), :]),
            C(Wo[256 * c:256 * (c + 1), :])], axis=0).astype(np.float32)
        in_maps.append({
            "x_in": x_fm,
            "wupqT": C(wupqT[256 * c:256 * (c + 1), :]).astype(np.int32),
            "wdnqT": C(wdnqT[1024 * c:1024 * (c + 1), :]).astype(np.int32),
            "s_up_p": sp, "s_dn_p": sdp,
            "wqkvo_sl": wqkvo_sl,
            "b_up_t": b_up_t, "b_dn_t": b_dn_t,
            "AupT_sl": C(AupT[:, 256 * c:256 * (c + 1)]).astype(np.float32),
            "Bup": C(Bup).astype(np.float32),
            "AdnT_sl": C(AdnT[:, 1024 * c:1024 * (c + 1)]).astype(np.float32),
            "Bdn": C(Bdn).astype(np.float32),
            "ones_col": ones_col, "ones_row": ones_row,
            "exp2": exp2, "exp2h": exp2h,
            "constv": constv_np,
        })
    return in_maps


N_PHASES = 3


def kernel(**inputs):
    in_maps = make_inputs(**inputs)
    nc = build_nc(N_PHASES)
    res = run_bass_kernel_spmd(nc, in_maps, core_ids=list(range(8)))
    out = np.zeros((2, 8192, 2048), np.float32)
    for c in range(8):
        b, sl = c // 4, c % 4
        out[b, sl * T:(sl + 1) * T, :] = res.results[c]["y_out"].T
    return out
